# version 02
# baseline (speedup 1.0000x reference)
import numpy as np
import ml_dtypes
import concourse.bass as bass
import concourse.mybir as mybir
from concourse.bass_utils import run_bass_kernel_spmd

F32 = mybir.dt.float32
BF16 = mybir.dt.bfloat16
AF = mybir.ActivationFunctionType
OP = mybir.AluOpType
AXX = mybir.AxisListType.X

T, D, NH, HD, DL, DI = 2048, 1024, 16, 64, 256, 4096
_DEV_NHEADS = None
_DEV_SERIAL = False
_DEV_RACECHK = False
_DEV_NOE = False
_DEV_NONORM = False
_DEV_TCUT = 99
_DEV_NOTAIL = False
_DEV_B1 = False
_DEV_QB2P0 = False
NC_ = 8

def _consts():
    bf = ml_dtypes.bfloat16
    t = np.arange(T, dtype=np.float64)
    inv = 10000.0 ** (-np.arange(0, HD, 2, dtype=np.float64) / HD)
    ang = np.outer(inv, t)
    c32, s32 = np.cos(ang), np.sin(ang)
    C = np.zeros((128, T), dtype=np.float64)
    S = np.zeros((128, T), dtype=np.float64)
    for blk in range(4):
        r = 32 * blk
        C[r:r + 32] = c32
        S[r:r + 32] = -s32 if blk % 2 == 0 else s32
    kk = np.arange(128)[:, None]
    E = np.zeros((128, 31 * 128), dtype=np.float64)
    for dlt in range(31):
        qq = np.arange(128)[None, :]
        d = 128 * (dlt - 15) + qq - kk
        f = -0.1 * np.log(np.abs(d) + 1.0) + np.log(np.exp(-np.maximum(d, 0) * 0.01) + 1e-9)
        E[:, 128 * dlt:128 * dlt + 128] = np.exp(f)
    GAM = 0.96875
    G = np.zeros((128, 17 * 128), dtype=np.float64)
    for dlt in range(-1, 16):
        qq = np.arange(128)[None, :]
        e = 128 * dlt + qq - kk
        G[:, 128 * (dlt + 1):128 * (dlt + 2)] = np.where(e >= 0, GAM ** e.astype(np.float64), 0.0)
    P = np.zeros((128, 128), dtype=np.float32)
    for m in range(128):
        p = m + 32 if (m % 64) < 32 else m - 32
        P[p, m] = 1.0
    I = np.eye(128, dtype=np.float32)
    def as_bf_f32(a):
        return a.astype(bf).astype(np.float32)
    return {
        "cC": as_bf_f32(C), "cS": as_bf_f32(S),
        "cE": as_bf_f32(E), "cG": as_bf_f32(G),
        "cP": P.astype(np.float32), "cI": I,
    }

_ENGS = ("T", "V", "S", "Y", "G")

class _Plan:
    def __init__(self):
        self.ops = []
        self.res = {}

    def op(self, eng, emit, reads=(), writes=(), dma=False, tag="",
           force_inc=False, shortw=False):
        opid = len(self.ops)
        deps = set()
        for r in reads:
            st = self.res.get(r)
            if st is not None and st["w"] is not None:
                deps.add(st["w"])
        for r in writes:
            st = self.res.get(r)
            if st is not None:
                if st["w"] is not None:
                    deps.add(st["w"])
                deps.update(st["r"])
        semkey = ("dma:" + writes[0]) if dma else eng
        if _DEV_SERIAL and opid > 0:
            deps.add(opid - 1)
        o = dict(id=opid, eng=eng, emit=emit, deps=deps, dma=dma,
                 need=force_inc or dma, val=None, tag=tag, semkey=semkey,
                 shortw=shortw)
        self.ops.append(o)
        for r in reads:
            self.res.setdefault(r, {"w": None, "r": []})["r"].append(opid)
        for r in writes:
            self.res[r] = {"w": opid, "r": []}
        return opid

    def claim(self, news, olds):
        pend = []
        for o in olds:
            st = self.res.get(o)
            if st is None:
                continue
            if st["w"] is not None:
                pend.append(st["w"])
            pend.extend(st["r"])
        for n in news:
            st = self.res.setdefault(n, {"w": None, "r": []})
            st["r"] = list(set(st["r"]) | set(pend))

    def finalize(self):
        for o in self.ops:
            for d in o["deps"]:
                p = self.ops[d]
                if p["eng"] != o["eng"] or p["dma"] or p["shortw"]:
                    p["need"] = True
        cnt = {}
        for o in self.ops:
            if o["need"]:
                k = o["semkey"]
                cnt[k] = cnt.get(k, 0) + (16 if o["dma"] else 1)
                o["val"] = cnt[k]
        self.final_counts = cnt

    def emit(self, nc, block, sems):
        plan = self
        def eng_fn(engname):
            def fn(eng):
                waited = {}
                for o in plan.ops:
                    if o["eng"] != engname:
                        continue
                    waits = {}
                    for d in o["deps"]:
                        p = plan.ops[d]
                        if p["eng"] == engname and not p["dma"] and not p["shortw"]:
                            continue
                        assert p["val"] is not None, (o["tag"], p["tag"])
                        k = p["semkey"]
                        waits[k] = max(waits.get(k, 0), p["val"])
                    for pe, v in sorted(waits.items()):
                        if waited.get(pe, 0) >= v:
                            continue
                        eng.wait_ge(sems[pe], v)
                        waited[pe] = v
                    inst = o["emit"](eng)
                    if o["need"]:
                        assert inst is not None, o["tag"]
                        inst.then_inc(sems[o["semkey"]], 16 if o["dma"] else 1)
                if engname == "Y":
                    for k, v in sorted(plan.final_counts.items()):
                        if k in ("dma:out", "dma:tapo"):
                            eng.wait_ge(sems[k], v)
            return fn
        block.tensor(lambda t: eng_fn("T")(t))
        block.vector(lambda v: eng_fn("V")(v))
        block.scalar(lambda s: eng_fn("S")(s))
        block.sync(lambda y: eng_fn("Y")(y))
        block.gpsimd(lambda g: eng_fn("G")(g))

class _BankGroups:
    def __init__(self):
        self.open = set()
    def flags(self, bank, last):
        start = bank not in self.open
        if start:
            self.open.add(bank)
        if last:
            self.open.discard(bank)
        return start, last

def _build(n_phases=99, tap=None, salt=None):
    nc = bass.Bass("TRN2", target_bir_lowering=False,
                   detect_race_conditions=_DEV_RACECHK)

    x_in = nc.dram_tensor("x", [T, D], F32, kind="ExternalInput")
    W = {}
    for nm, shp in [
        ("aWq", [D, D]), ("abq", [D]), ("aWkd", [D, DL]), ("abkd", [DL]),
        ("aWku", [DL, 2 * D]), ("abku", [2 * D]), ("aWo", [D, D]), ("abo", [D]),
        ("bWq", [D, D]), ("bbq", [D]), ("bWk", [D, D]), ("bbk", [D]),
        ("bWv", [D, D]),
        ("fW1", [D, DI]), ("fb1", [DI]), ("fW2", [DI, D]), ("fb2", [D]),
    ]:
        W[nm] = nc.dram_tensor(nm, shp, F32, kind="ExternalInput")
    CN = {}
    for nm, shp in [("cC", [128, T]), ("cS", [128, T]), ("cE", [128, 31 * 128]),
                    ("cG", [128, 17 * 128]), ("cP", [128, 128]), ("cI", [128, 128])]:
        CN[nm] = nc.dram_tensor(nm, shp, F32, kind="ExternalInput")
    out_t = nc.dram_tensor("out", [T, D], F32, kind="ExternalOutput")
    tap_t = None
    if tap is not None:
        tap_t = nc.dram_tensor("tapout", list(tap[1]), tap[2], kind="ExternalOutput")

    SB0 = 16384
    def at(name, shape, dtype, offset):
        return nc.alloc_sbuf_tensor_at(name, shape, dtype, offset=SB0 + offset)
    x_fm   = at("x_fm", [128, 8 * T], F32, 0)
    xn     = at("xn", [128, 8 * T], BF16, 65536)
    r2_bf  = at("r2_bf", [128, 8 * T], BF16, 98304)
    r2_f32 = at("r2_f32", [128, 4 * T], F32, 98304)
    _wo0   = at("woSl0", [128, D], BF16, 131072)
    woSl   = [_wo0, _wo0]
    bvr    = at("bvr", [1, 1024], BF16, 133120)
    aWkd_s = at("aWkd_s", [128, 2048], BF16, 135168)
    aWku_s = at("aWku_s", [128, 4096], BF16, 139264)
    C_s    = at("C_s", [128, T], BF16, 147456)
    S_s    = at("S_s", [128, T], BF16, 151552)
    E_s    = at("E_s", [128, 31 * 128], BF16, 155648)
    onesel = at("onesel", [65, 64], BF16, 163584)
    kmax2  = at("kmax2", [128, 4], F32, 163712)
    mkf    = at("mkf", [64, 2], F32, 163744)
    mkcol  = at("mkcol", [64, 2], BF16, 163776)
    fW1b   = [at("fW1a", [128, D], BF16, 131072),
              at("fW1b", [128, D], BF16, 133120)]
    fW2b   = [at("fW2a", [128, 4096], BF16, 135168),
              at("fW2b", [128, 4096], BF16, 143360)]
    bWk_s  = at("bWk_s", [128, 8 * D], BF16, 131072)
    A_s    = at("A_s", [128, 8 * D], BF16, 147456)
    vtm    = at("vtm", [128, 16 * D], BF16, 131072)
    repA   = at("repA", [128, T], F32, 163840)
    repB   = at("repB", [128, T], F32, 172032)
    q_aug  = [at("q_aug0", [65, T], BF16, 163840),
              at("q_aug1", [65, T], BF16, 167936)]
    k_aug  = [at("k_aug0", [65, T], BF16, 172032),
              at("k_aug1", [65, T], BF16, 176128)]
    scr    = at("scr", [64, T], BF16, 180224)
    wvStg  = at("wvStg", [128, D], F32, 163840)
    wvT2   = [at("wvT2a", [128, 128], BF16, 167936),
              at("wvT2b", [128, 128], BF16, 168192)]
    bqSl   = [at("bqSl0", [128, D], BF16, 168448),
              at("bqSl1", [128, D], BF16, 170496)]
    kvd_s  = at("kvd_s", [128, 2 * T], BF16, 184320)
    Gt_s   = at("Gt_s", [128, 17 * 128], BF16, 184320)
    raug   = at("raug", [2, T], BF16, 188672)
    v_aug  = [at("v_aug0", [128, 16 * 68], BF16, 193024),
              at("v_aug1", [128, 16 * 68], BF16, 195200)]
    attE   = [at("attE0", [128, 512], BF16, 197376),
              at("attE1", [128, 512], BF16, 198400),
              at("attE2", [128, 512], BF16, 199424)]
    qaug2  = at("qaug2", [2, T], BF16, 193024)
    _al = lambda n: (n + 31) & ~31
    m0 = 200448
    def atm(name, shape, dtype, size):
        nonlocal m0
        h = at(name, shape, dtype, m0)
        m0 += _al(size)
        return h
    rowa   = atm("rowa", [128, 1024], F32, 4096)
    qscr   = at("qscr", [64, 2048], BF16, 200448)
    sglb_off = m0
    sglb   = atm("sglb", [128, 1024], BF16, 2048)
    sglb2  = atm("sglb2", [128, 1024], BF16, 2048)
    repsb  = atm("repsb", [64, 512], BF16, 1024)
    nztmp  = atm("nztmp", [64, 512], BF16, 1024)
    ident  = atm("ident", [128, 128], F32, 512)
    prot   = atm("prot", [64, 64], BF16, 128)
    ones_b = atm("ones_b", [128, 128], BF16, 256)
    ones_f = atm("ones_f", [128, 128], F32, 512)
    bqc64  = atm("bqc64", [64, 16], F32, 64)
    bkuk   = atm("bkuk", [64, 16], F32, 64)
    bkdc   = atm("bkdc", [128, 2], F32, 8)
    bvc    = atm("bvc", [128, 8], BF16, 16)
    boc    = atm("boc", [128, 8], F32, 32)
    bopc   = atm("bopc", [128, 8], F32, 32)
    b1c    = atm("b1c", [128, 32], F32, 128)
    b2c    = atm("b2c", [128, 8], F32, 32)
    bqbc   = atm("bqbc", [128, 8], BF16, 16)
    bkbc   = atm("bkbc", [128, 8], BF16, 16)
    ucol   = atm("ucol", [128, 8], BF16, 16)
    wcol   = atm("wcol", [128, 8], BF16, 16)
    mvrow  = atm("mvrow", [128, 136], BF16, 272)




    cst    = atm("cst", [128, 8], F32, 32)
    epsc, tinyc, onec, safc, s0c, zeroc, ln02c = (cst[:, i:i + 1] for i in range(7))
    assert SB0 + m0 <= 229344, m0

    ps = nc.alloc_psum_tensor("ps", [128, 4096], F32)
    def pb(i, p0=0, p1=128, c0=0, cn=512):
        assert c0 + cn <= 512, (c0, cn)
        return ps[p0:p1, 512 * i + c0:512 * i + c0 + cn]

    P = _Plan()
    BG = _BankGroups()
    if salt is not None:
        P.op("G", lambda e, v=float(salt): e.memset(rowa[0:1, 768:1024], v),
             writes=["salt"], tag="salt", force_inc=True)

    def mm(bank, out_ap, lhsT, rhs, first, last, reads, tile_position=None,
           tag="", dup=False):
        assert not dup or (first and last)
        for _ in range(2 if dup else 1):
            start, stop = BG.flags(bank, last)
            def em(e, o=out_ap, l=lhsT, r=rhs, st=start, sp=stop, tp=tile_position):
                return e.matmul(o, l, r, start=st, stop=sp, tile_position=tp)
            P.op("T", em, reads=list(reads), writes=[f"pb{bank}"], tag=tag or "mm")

    def dmaY(out_ap, in_ap, reads, writes, tag="", force_inc=False, slow=False):
        P.op("Y", lambda e, o=out_ap, i=in_ap, sl=slow:
             e.dma_start(out=o, in_=i, allow_slow_non_contiguous=sl),
             reads=list(reads), writes=list(writes), dma=True, tag=tag or "dmaY",
             force_inc=force_inc)

    def dmaG(out_ap, in_ap, reads, writes, tag="", force_inc=False, slow=False):
        P.op("G", lambda e, o=out_ap, i=in_ap, sl=slow:
             e.dma_start(out=o, in_=i, allow_slow_non_contiguous=sl),
             reads=list(reads), writes=list(writes), dma=True, tag=tag or "dmaG",
             force_inc=force_inc)

    def vop(emit, reads, writes, tag="", dup=False, shortw=False):
        for _ in range(2 if dup else 1):
            P.op("V", emit, reads=list(reads), writes=list(writes),
                 tag=tag or "v", shortw=shortw)

    def sop(emit, reads, writes, tag="", dup=False, shortw=False):
        for _ in range(2 if dup else 1):
            P.op("S", emit, reads=list(reads), writes=list(writes),
                 tag=tag or "s", shortw=shortw)

    XC = lambda c: x_fm[:, 2048 * c:2048 * c + 2048]
    XNC = lambda c: xn[:, 2048 * c:2048 * c + 2048]
    R2B = lambda c: r2_bf[:, 2048 * c:2048 * c + 2048]
    ALLX = [f"x{c}" for c in range(8)]
    ALLXN = [f"xn{c}" for c in range(8)]

    for nm, dst in [("cC", C_s), ("cS", S_s)]:
        dmaG(dst[0:64, :], CN[nm].ap()[0:64, :], reads=[], writes=[nm[1:] + "_t"],
             tag="ld" + nm)
    dmaG(E_s[:, :], CN["cE"].ap()[:, :], reads=[], writes=["E_t"], tag="ldcE")
    dmaG(prot[:, :], CN["cP"].ap()[0:64, 0:64], reads=[], writes=["P_t"], tag="ldcP")
    dmaY(ident[:, :], CN["cI"].ap()[:, :], reads=[], writes=["I_t"], tag="ldcI")
    P.op("G", lambda e: e.memset(ones_b[:, :], 1.0), writes=["ones_b"])
    P.op("G", lambda e: e.memset(ones_f[:, :], 1.0), writes=["ones_f"])
    P.op("G", lambda e: e.memset(epsc[:, :], 1e-5), writes=["epsc"])
    P.op("G", lambda e: e.memset(tinyc[:, :], 1e-35), writes=["tinyc"])
    P.op("G", lambda e: e.memset(onec[:, :], 1.0), writes=["onec"])
    P.op("G", lambda e: e.memset(safc[:, :], 0.1), writes=["safc"])
    P.op("G", lambda e: e.memset(zeroc[:, :], 0.0), writes=["zeroc"])
    P.op("G", lambda e: e.memset(onesel[:, :], 0.0), writes=["onesel"])
    P.op("G", lambda e: e.memset(onesel[64:65, :], 1.0), writes=["onesel"])
    P.op("G", lambda e: e.memset(sglb[:, :], 0.0), writes=["sgl", "repsb", "nzt",
                                                           "stb0", "stb1"])
    P.op("G", lambda e: e.memset(sglb2[:, :], 0.0), writes=["sgl2"])
    P.op("G", lambda e: e.memset(ln02c[:, :], -1.6094379124341003), writes=["ln02c"])

    dmaY(bass.AP(bqc64, 0, [[16, 64], [1, 16]]),
         bass.AP(W["abq"], 0, [[1, 64], [64, 16]]), reads=[], writes=["bqc64"],
         tag="ldbq", slow=True)
    vop(lambda e: e.tensor_scalar_mul(bqc64[:, :], bqc64[:, :], 0.125),
        reads=["bqc64"], writes=["bqc64"], tag="sclbq", shortw=True)
    P.op("G", lambda e: e.memset(bkuk[:, :], 0.0), writes=["bkuk"])
    dmaY(bass.AP(bkuk, 0, [[16, 64], [1, 16]]),
         bass.AP(W["abku"], 0, [[1, 64], [128, 16]]), reads=["bkuk"], writes=["bkuk"],
         tag="ldbkuk", slow=True)
    dmaY(bass.AP(bkdc, 0, [[2, 128], [1, 2]]),
         bass.AP(W["abkd"], 0, [[1, 128], [128, 2]]), reads=[], writes=["bkdc"],
         tag="ldbkd", slow=True)
    for g in range(2):
        dmaG(bass.AP(bvc, 8 * 64 * g, [[8, 64], [1, 8]]),
             bass.AP(W["abku"], 64 + 128 * g, [[1, 64], [256, 8]]),
             reads=["bvc"], writes=["bvc"], tag="ldbv", slow=True)
    dmaG(bass.AP(bvr, 0, [[1024, 1], [64, 16], [1, 64]]),
         bass.AP(W["abku"], 64, [[1, 1], [128, 16], [1, 64]]),
         reads=[], writes=["bvr"], tag="ldbvr", slow=True)
    dmaY(bass.AP(boc, 0, [[8, 128], [1, 8]]),
         bass.AP(W["abo"], 0, [[1, 128], [128, 8]]), reads=[], writes=["boc"],
         tag="ldbo", slow=True)
    dmaY(bass.AP(b1c, 0, [[32, 128], [1, 32]]),
         bass.AP(W["fb1"], 0, [[1, 128], [128, 32]]), reads=[], writes=["b1c"],
         tag="ldb1", slow=True)
    dmaY(bass.AP(b2c, 0, [[8, 128], [1, 8]]),
         bass.AP(W["fb2"], 0, [[1, 128], [128, 8]]), reads=[], writes=["b2c"],
         tag="ldb2", slow=True)
    dmaG(bass.AP(bqbc, 0, [[8, 128], [1, 8]]),
         bass.AP(W["bbq"], 0, [[1, 128], [128, 8]]), reads=[], writes=["bqbc"],
         tag="ldbqb", slow=True)
    dmaG(bass.AP(bkbc, 0, [[8, 128], [1, 8]]),
         bass.AP(W["bbk"], 0, [[1, 128], [128, 8]]), reads=[], writes=["bkbc"],
         tag="ldbkb", slow=True)

    dmaG(bass.AP(aWkd_s, 0, [[2048, 128], [256, 8], [1, 256]]),
         bass.AP(W["aWkd"], 0, [[256, 128], [32768, 8], [1, 256]]),
         reads=[], writes=["aWkd_s"], tag="ldaWkd")
    dmaG(bass.AP(aWku_s, 0, [[4096, 128], [2048, 2], [1, 2048]]),
         bass.AP(W["aWku"], 0, [[2048, 128], [262144, 2], [1, 2048]]),
         reads=[], writes=["aWku_s"], tag="ldaWku")

    xt_v = [r2_f32[:, 0:1024], r2_f32[:, 1024:2048]]
    for tt in range(16):
        sl = tt % 2
        xt = xt_v[sl]
        dmaY(xt, x_in.ap()[128 * tt:128 * tt + 128, :],
             reads=[], writes=[f"xt{sl}"], tag=f"ldx{tt}")
        for fc in range(8):
            b = fc % 4
            P.op("T", lambda e, o=pb(b, cn=128), i=xt[:, 128 * fc:128 * fc + 128]:
                 e.transpose(o, i, ident[:, :]),
                 reads=[f"xt{sl}", "I_t"], writes=[f"pb{b}"], tag="xT")
            vop(lambda e, o=x_fm[:, 2048 * fc + 128 * tt:2048 * fc + 128 * tt + 128],
                i=pb(b, cn=128): e.tensor_copy(o, i),
                reads=[f"pb{b}"], writes=[f"x{fc}"], tag="xTe")

    LNTMP = [r2_f32[:, 2048 * i:2048 * i + 2048] for i in range(4)]
    XBF = lambda c: r2_bf[:, 2048 * c:2048 * c + 2048]
    R2_PREV = ["xt0", "xt1", "aout", "hact", "qprime"]
    R4_PREV = ["qga0", "qga1", "kga0", "kga1", "scr", "WvT", "wvStg"]

    def layernorm(ln_id):
        P.claim(["xbf", "lntmp0", "lntmp1", "lntmp2", "lntmp3"], R2_PREV)
        P.claim(["repA", "repB"], R4_PREV)
        for c in range(8):
            vop(lambda e, o=XBF(c), i=XC(c): e.tensor_copy(o, i),
                reads=[f"x{c}"], writes=["xbf"], tag=f"ln{ln_id}cast")
        for s in range(4):
            for c in range(8):
                mm(s, ps[0:1, 512 * s:512 * s + 512], ones_b[:, 0:1],
                   r2_bf[:, 2048 * c + 512 * s:2048 * c + 512 * s + 512],
                   first=(c == 0), last=(c == 7), reads=["ones_b", "xbf"],
                   tag=f"ln{ln_id}sx")
        for c in range(8):
            vop(lambda e, o=XNC(c), i=XBF(c): e.tensor_mul(o, i, i),
                reads=["xbf"], writes=["xnsq"], tag=f"ln{ln_id}sq")
        for s in range(4):
            for c in range(8):
                mm(4 + s, ps[0:1, 2048 + 512 * s:2048 + 512 * s + 512], ones_b[:, 0:1],
                   xn[:, 2048 * c + 512 * s:2048 * c + 512 * s + 512],
                   first=(c == 0), last=(c == 7), reads=["ones_b", "xnsq"],
                   tag=f"ln{ln_id}sx2")
        P.claim(["rw0", "rw1", "rw2"], ["xbf"])
        RW = [r2_f32[0:1, 0:2048], r2_f32[0:1, 2048:4096], r2_f32[0:1, 4096:6144]]
        vop(lambda e: e.tensor_scalar_mul(RW[0], ps[0:1, 0:2048], 1.0 / D),
            reads=["pb0", "pb1", "pb2", "pb3"], writes=["rw0"], tag=f"ln{ln_id}m")
        vop(lambda e: e.tensor_scalar_mul(RW[1], ps[0:1, 2048:4096], 1.0 / D),
            reads=["pb4", "pb5", "pb6", "pb7"], writes=["rw1"], tag=f"ln{ln_id}s2")
        vop(lambda e: e.tensor_mul(RW[2], RW[0], RW[0]),
            reads=["rw0"], writes=["rw2"], tag=f"ln{ln_id}m2")
        vop(lambda e: e.tensor_sub(RW[1], RW[1], RW[2]),
            reads=["rw1", "rw2"], writes=["rw1"], tag=f"ln{ln_id}var")
        sop(lambda e: e.activation(RW[2], RW[1], AF.Ln,
                                   bias=epsc[0:1, :], scale=1.0),
            reads=["rw1", "epsc"], writes=["rw2"], tag=f"ln{ln_id}ln")
        sop(lambda e: e.activation(RW[1], RW[2], AF.Exp,
                                   bias=zeroc[0:1, :], scale=-0.5),
            reads=["rw2", "zeroc"], writes=["rw1"], tag=f"ln{ln_id}exp")
        vop(lambda e: e.tensor_scalar_mul(RW[2], RW[0], -1.0),
            reads=["rw0"], writes=["rw2"], tag=f"ln{ln_id}negm")
        for nm, row, dst, dk in [("A", RW[1], repA, "rw1"),
                                 ("B", RW[2], repB, "rw2")]:
            for s in range(4):
                b = 4 + s
                mm(b, pb(b), ones_f[0:1, 0:128], row[:, 512 * s:512 * s + 512],
                   first=True, last=True, reads=["ones_f", dk],
                   tag=f"ln{ln_id}rep{nm}")
                vop(lambda e, o=dst[:, 512 * s:512 * s + 512], i=pb(b): e.tensor_copy(o, i),
                    reads=[f"pb{b}"], writes=[f"rep{nm}"], tag=f"ln{ln_id}repe{nm}")
        P.claim(["lntmp0", "lntmp1", "lntmp2", "lntmp3"], ["rw0", "rw1", "rw2"])
        for c in range(8):
            tmp = LNTMP[c % 4]
            vop(lambda e, o=tmp, a=XC(c): e.tensor_add(o, a, repB[:, :]),
                reads=[f"x{c}", "repB", f"lntmp{c % 4}"], writes=[f"lntmp{c % 4}"],
                tag=f"ln{ln_id}a")
            vop(lambda e, o=XNC(c), a=tmp: e.tensor_mul(o, a, repA[:, :]),
                reads=[f"lntmp{c % 4}", "repA", "xnsq"], writes=[f"xn{c}", "xnsq"],
                tag=f"ln{ln_id}b")

    def tap_sbuf(name, tensor, keys, cols):
        if tap is not None and tap[0] == name:
            for c0 in range(0, cols, 2048):
                dmaY(tap_t.ap()[:, c0:c0 + 2048], tensor[:, c0:c0 + 2048],
                     reads=list(keys), writes=["tapo"], tag="tap", force_inc=True)
            return True
        return False

    def tap_x(name):
        if tap is not None and tap[0] == name:
            for c in range(8):
                dmaY(tap_t.ap()[:, 2048 * c:2048 * c + 2048], XC(c),
                     reads=[f"x{c}"], writes=["tapo"], tag="tap", force_inc=True)
            return True
        return False

    def tap_xn(name):
        if tap is not None and tap[0] == name:
            for c in range(8):
                dmaY(tap_t.ap()[:, 2048 * c:2048 * c + 2048], XNC(c),
                     reads=[f"xn{c}"], writes=["tapo"], tag="tap", force_inc=True)
            return True
        return False

    def attn_a():
        P.claim(["qga0", "qga1", "kga0", "kga1", "scr", "qscr"], ["repA", "repB"])
        P.claim(["aout"], ["xbf", "lntmp0", "lntmp1", "lntmp2", "lntmp3", "xt0", "xt1"])
        for sl in range(2):
            P.op("G", lambda e, t=v_aug[sl]: e.memset(t[:, :], 1.0),
                 writes=[f"vga{sl}"], tag="vone")
        vop(lambda e: e.tensor_scalar_mul(bvr[0:1, :], bvr[0:1, :], 0.2),
            reads=["bvr"], writes=["bvr"], tag="bvr02", shortw=True)

        for lc in range(2):
            for ts in range(4):
                b = 2 + (ts % 2)
                for dm in range(8):
                    mm(b, pb(b), aWkd_s[:, 256 * dm + 128 * lc:256 * dm + 128 * lc + 128],
                       XNC(dm)[:, 512 * ts:512 * ts + 512],
                       first=(dm == 0), last=(dm == 7),
                       reads=["aWkd_s", f"xn{dm}"], tag="kvd")
                sop(lambda e, o=kvd_s[:, 2048 * lc + 512 * ts:2048 * lc + 512 * ts + 512],
                    i=pb(b), lc=lc: e.activation(o, i, AF.Identity, bias=bkdc[:, lc:lc + 1],
                                                 scale=1.0),
                    reads=[f"pb{b}", "bkdc"], writes=["kvd"], tag="kvde")

        def prep_gen(h):
            sl = h % 2
            qa, ka, va = q_aug[sl], k_aug[sl], v_aug[sl]
            qk, kk_, vk = f"qga{sl}", f"kga{sl}", f"vga{sl}"
            wqk = f"wqSl{sl}"
            wqb = 512 * sl
            dmaG(bass.AP(woSl[0], wqb, [[1024, 128], [64, 8], [1, 64]]),
                 bass.AP(W["aWq"], 64 * h, [[1024, 128], [131072, 8], [1, 64]]),
                 reads=[], writes=[wqk], tag=f"ldwq{h}")
            yield
            for half in range(2):
                hb = 1024 * half
                for sub in range(2):
                    b = 2 + sub
                    for lc in range(2):
                        mm(b, pb(b, p1=64),
                           aWku_s[:, 2048 * lc + 128 * h:2048 * lc + 128 * h + 64],
                           kvd_s[:, 2048 * lc + hb + 512 * sub:2048 * lc + hb + 512 * sub + 512],
                           first=(lc == 0), last=(lc == 1),
                           reads=["aWku_s", "kvd"], tag=f"kp{h}")
                    sop(lambda e, o=scr[0:64, hb + 512 * sub:hb + 512 * sub + 512],
                        i=pb(b, p1=64), h=h:
                        e.activation(o, i, AF.Identity, bias=bkuk[:, h:h + 1], scale=1.0),
                        reads=[f"pb{b}", "bkuk"], writes=["scr"], tag=f"kpe{h}")
                yield
                for sub in range(2):
                    b = 2 + sub
                    mm(b, pb(b, p1=64), prot[:, :],
                       scr[0:64, hb + 512 * sub:hb + 512 * sub + 512],
                       first=True, last=True, reads=["P_t", "scr"], tag=f"krot{h}")
                vop(lambda e, o=ka[0:64, hb:hb + 1024], i=scr[0:64, hb:hb + 1024],
                    c=C_s[0:64, hb:hb + 1024]: e.tensor_mul(o, i, c),
                    reads=["scr", "C_t"], writes=[kk_], tag=f"kr1{h}")
                vop(lambda e, o=scr[0:64, hb:hb + 1024],
                    i=ps[0:64, 1024:2048], s=S_s[0:64, hb:hb + 1024]:
                    e.tensor_mul(o, i, s),
                    reads=["pb2", "pb3", "S_t"], writes=["scr"], tag=f"kr2{h}")
                vop(lambda e, o=ka[0:64, hb:hb + 1024], i=scr[0:64, hb:hb + 1024]:
                    e.tensor_add(o, o, i),
                    reads=["scr", kk_], writes=[kk_], tag=f"kr3{h}")
                yield
            vop(lambda e, sl=sl: e.tensor_reduce(mkf[:, sl:sl + 1], ka[0:64, :],
                                                 AXX, OP.add),
                reads=[kk_], writes=[f"mkf{sl}"], tag=f"mk{h}", shortw=True)
            vop(lambda e, sl=sl: e.tensor_copy(mkcol[:, sl:sl + 1], mkf[:, sl:sl + 1]),
                reads=[f"mkf{sl}"], writes=[f"mk{sl}"], tag=f"mkc{h}", shortw=True)
            yield
            for half in range(2):
                hb = 1024 * half
                for sub in range(2):
                    b = 2 + sub
                    for dm in range(8):
                        mm(b, pb(b, p1=64),
                           woSl[0][:, wqb + 64 * dm:wqb + 64 * dm + 64],
                           XNC(dm)[:, hb + 512 * sub:hb + 512 * sub + 512],
                           first=(dm == 0), last=(dm == 7),
                           reads=[wqk, f"xn{dm}"], tag=f"qp{h}")
                    sop(lambda e, o=qscr[0:64, hb + 512 * sub:hb + 512 * sub + 512],
                        i=pb(b, p1=64), h=h:
                        e.activation(o, i, AF.Identity, bias=bqc64[:, h:h + 1], scale=0.125),
                        reads=[f"pb{b}", "bqc64"], writes=["qscr"], tag=f"qpe{h}")
                    yield
                for sub in range(2):
                    b = 2 + sub
                    mm(b, pb(b, p1=64), prot[:, :],
                       qscr[0:64, hb + 512 * sub:hb + 512 * sub + 512],
                       first=True, last=True, reads=["P_t", "qscr"], tag=f"qrot{h}")
                vop(lambda e, o=qa[0:64, hb:hb + 1024], i=qscr[0:64, hb:hb + 1024],
                    c=C_s[0:64, hb:hb + 1024]: e.tensor_mul(o, i, c),
                    reads=["qscr", "C_t"], writes=[qk], tag=f"qr1{h}")
                vop(lambda e, o=qscr[0:64, hb:hb + 1024],
                    i=ps[0:64, 1024:2048], s=S_s[0:64, hb:hb + 1024]:
                    e.tensor_mul(o, i, s),
                    reads=["pb2", "pb3", "S_t"], writes=["qscr"], tag=f"qr2{h}")
                vop(lambda e, o=qa[0:64, hb:hb + 1024], i=qscr[0:64, hb:hb + 1024]:
                    e.tensor_add(o, o, i),
                    reads=["qscr", qk], writes=[qk], tag=f"qr3{h}")
                yield
            for tt_ in range(16):
                b = 2 + tt_ // 8
                for lc in range(2):
                    mm(b, pb(b, c0=64 * (tt_ % 8), cn=64),
                       kvd_s[:, 2048 * lc + 128 * tt_:2048 * lc + 128 * tt_ + 128],
                       aWku_s[:, 2048 * lc + 128 * h + 64:2048 * lc + 128 * h + 128],
                       first=(lc == 0), last=(lc == 1),
                       reads=["kvd", "aWku_s"], tag=f"vp{h}")
            vop(lambda e, o=bass.AP(va, 0, [[1088, 128], [68, 16], [1, 64]]),
                i=ps[0:128, 1024:2048]: e.tensor_copy(o, i),
                reads=["pb2", "pb3"], writes=[vk], tag=f"vpe{h}")
            yield
            for tt_ in range(16):
                mm(2, ps[0:1, 1024:1024 + 65], ones_b[:, 0:1],
                   va[:, 68 * tt_:68 * tt_ + 65],
                   first=(tt_ == 0), last=(tt_ == 15), reads=["ones_b", vk],
                   tag=f"mv{h}")
            vop(lambda e, sl=sl: e.tensor_scalar_mul(mvrow[0:1, 68 * sl:68 * sl + 65],
                                              ps[0:1, 1024:1024 + 65], 0.2 / T),
                reads=["pb2"], writes=[f"mv{sl}"], tag=f"mve{h}", shortw=True)
            vop(lambda e, sl=sl, h=h: e.tensor_add(
                    mvrow[0:1, 68 * sl:68 * sl + 64],
                    mvrow[0:1, 68 * sl:68 * sl + 64], bvr[0:1, 64 * h:64 * h + 64]),
                reads=[f"mv{sl}", "bvr"], writes=[f"mv{sl}"], tag=f"mvb{h}", shortw=True)
            yield

        def attention(h, pgen):
            sl = h % 2
            qa, ka, va = q_aug[sl], k_aug[sl], v_aug[sl]
            qk, kk_, vk = f"qga{sl}", f"kga{sl}", f"vga{sl}"

            def issue_st(t):
                qp, kb, qs = t // 32, (t // 2) % 16, t % 2
                sb = t % 2
                slot = t % 3
                qc0 = 1024 * qp + 512 * qs
                mm(sb, pb(sb), ka[0:64, 128 * kb:128 * kb + 128],
                   qa[0:64, qc0:qc0 + 512], first=True, last=True,
                   reads=[kk_, qk], tag=f"st{h}")
                sop(lambda e, o=attE[slot][:, :], i=pb(sb):
                    e.activation(o, i, AF.Exp, bias=zeroc[:, :], scale=1.0),
                    reads=[f"pb{sb}", "zeroc"], writes=[f"attE{slot}"],
                    tag=f"ex{h}")
                d0 = (8 * qp + 4 * qs - kb) + 15
                if not _DEV_NOE:
                    vop(lambda e, o=attE[slot][:, :],
                        ee=E_s[:, 128 * d0:128 * d0 + 512]: e.tensor_mul(o, o, ee),
                        reads=[f"attE{slot}", "E_t"], writes=[f"attE{slot}"],
                        tag=f"em{h}")

            def issue_av(t):
                qp, kb, qs = t // 32, (t // 2) % 16, t % 2
                ob = 4 + 2 * qp + qs
                slot = t % 3
                mm(ob, pb(ob, p1=65), va[:, 68 * kb:68 * kb + 65],
                   attE[slot][:, :], first=(kb == 0), last=(kb == 15),
                   reads=[vk, f"attE{slot}"], tag=f"av{h}")

            def tail(qp):
                ob0 = 4 + 2 * qp
                qb0 = 1024 * qp
                for qs in range(2):
                    b = 2 + qs
                    mm(b, ps[0:1, 512 * b:512 * b + 512], mkcol[:, sl:sl + 1],
                       qa[0:64, qb0 + 512 * qs:qb0 + 512 * qs + 512],
                       first=True, last=True, reads=[f"mk{sl}", qk],
                       tag=f"ms{h}")
                sop(lambda e: e.activation(sglb[0:1, 0:1024], ps[0:1, 1024:2048],
                                           AF.Sigmoid, bias=zeroc[0:1, :],
                                           scale=8.0 / T),
                    reads=["pb2", "pb3", "zeroc", "sgl"], writes=["sgl"],
                    tag=f"sg{h}")
                vop(lambda e, i=ps[64:65, 512 * ob0:512 * ob0 + 1024]:
                    e.tensor_copy(sglb2[64:65, 0:1024], i),
                    reads=[f"pb{ob0}", f"pb{ob0 + 1}", "sgl2"], writes=["sgl2"],
                    tag=f"le{h}")
                for qs in range(2):
                    ob = ob0 + qs
                    mm(2, pb(2, p1=64), onesel[:, :],
                       sglb2[0:65, 512 * qs:512 * qs + 512],
                       first=True, last=True, reads=["onesel", "sgl2"],
                       tag=f"rp{h}")
                    sop(lambda e: e.activation(pb(3, p1=64), pb(2, p1=64),
                                               AF.Ln, bias=tinyc[0:64, :], scale=1.0),
                        reads=["pb2", "tinyc"], writes=["pb3"], tag=f"lr{h}")
                    sop(lambda e: e.activation(repsb[:, :], pb(3, p1=64),
                                               AF.Exp, bias=zeroc[0:64, :], scale=-1.0),
                        reads=["pb3", "zeroc", "repsb"], writes=["repsb"],
                        tag=f"rc{h}")
                    mm(3, pb(3, p1=64), mvrow[0:1, 68 * sl:68 * sl + 64],
                       sglb[0:1, 512 * qs:512 * qs + 512],
                       first=True, last=True, reads=[f"mv{sl}", "sgl"],
                       tag=f"ia{h}")
                    c0 = 2048 * (h // 2) + qb0 + 512 * qs
                    if h % 2 == 0:
                        vop(lambda e, o=r2_bf[0:64, c0:c0 + 512],
                            i=pb(ob, p1=64): e.tensor_mul(o, i, repsb[:, :]),
                            reads=[f"pb{ob}", "repsb"], writes=["aout"], tag=f"nz{h}")
                        vop(lambda e, o=r2_bf[0:64, c0:c0 + 512],
                            i2=pb(3, p1=64):
                            e.tensor_add(o, o, i2),
                            reads=["pb3", "aout"],
                            writes=["aout"], tag=f"nza{h}")
                    else:
                        vop(lambda e, i=pb(ob, p1=64):
                            e.tensor_mul(nztmp[:, :], i, repsb[:, :]),
                            reads=[f"pb{ob}", "repsb"], writes=["nzt"], tag=f"nz{h}")
                        vop(lambda e, i2=pb(3, p1=64):
                            e.tensor_add(nztmp[:, :], nztmp[:, :], i2),
                            reads=["pb3", "nzt"],
                            writes=["nzt"], tag=f"nza{h}")
                        dmaG(r2_bf[64:128, c0:c0 + 512], nztmp[:, :],
                             reads=["nzt"], writes=["aout"], tag=f"nzc{h}")

            LAG = 2
            for t in range(64):
                issue_st(t)
                if t % 4 == 2:
                    next(pgen, None)
                if t >= LAG:
                    issue_av(t - LAG)
                    if t - LAG == 31:
                        tail(0)
            for t in range(64 - LAG, 64):
                issue_av(t)
            for _ in pgen:
                pass
            tail(1)

        nh = _DEV_NHEADS or NH
        if tap is not None and tap[0] == "kvd":
            tap_sbuf("kvd", kvd_s, ["kvd"], 4096)
            return
        for _ in prep_gen(0):
            pass
        for h in range(nh):
            pgen = prep_gen(h + 1) if h + 1 < nh else iter(())
            attention(h, pgen)

        if tap is not None and tap[0] == "ps0":
            for b in range(8):
                vop(lambda e, b=b: e.tensor_copy(
                        r2_f32[:, 4096 + 512 * b:4096 + 512 * b + 512],
                        ps[:, 512 * b:512 * b + 512]),
                    reads=[f"pb{b}"], writes=["tapstg"], tag="tapc")
            for c0 in range(0, 4096, 2048):
                dmaY(tap_t.ap()[:, c0:c0 + 2048], r2_f32[:, 4096 + c0:6144 + c0],
                     reads=["tapstg"], writes=["tapo"], tag="tap",
                     force_inc=True)
            return
        if tap is not None and tap[0] == "tail0":
            stg = r2_f32[:, 4096:8192]
            def cp(dst0, src_ap, key):
                vop(lambda e,
                    o=stg[0:src_ap.shape[0], dst0:dst0 + src_ap.shape[-1]],
                    i=src_ap:
                    e.tensor_copy(o, i), reads=[key], writes=["tapstg"],
                    tag="tapc")
            cp(0, rowa[0:1, :], "rw64")
            cp(1024, sglb[0:1, :], "sgl")
            cp(2048, sglb2[64:65, :], "sgl2")
            cp(3072, mvrow[0:1, 0:64], "mv0")
            cp(3250, mkcol[0:64, 0:1], "mk0")
            cp(3252, mkf[0:64, 0:1], "mkf0")
            cp(3260, kmax2[0:1, 0:1], "km20")
            cp(3262, kmax2[0:1, 2:4], "km2s")
            cp(3270, onesel[0:65, :], "onesel")
            cp(3300, repsb[0:64, :], "repsb")
            for c0 in range(0, 4096, 2048):
                dmaY(tap_t.ap()[:, c0:c0 + 2048], stg[:, c0:c0 + 2048],
                     reads=["tapstg"], writes=["tapo"], tag="tap",
                     force_inc=True)
            return
        if tap is not None and tap[0] == "scr":
            dmaY(tap_t.ap()[0:64, :], scr[0:64, 0:2048],
                 reads=["scr"], writes=["tapo"], tag="tap", force_inc=True)
            return
        if tap is not None and tap[0] == "km":
            dmaY(tap_t.ap()[:, 0:4], kmax2[:, :],
                 reads=["km20", "km2s"], writes=["tapo"], tag="tap", force_inc=True, slow=True)
            return
        if tap is not None and tap[0] == "rwa":
            dmaY(tap_t.ap()[:, 0:1024], rowa[:, :],
                 reads=["rw64"], writes=["tapo"], tag="tap", force_inc=True)
            return
        if tap is not None and tap[0] == "wq0":
            dmaY(tap_t.ap()[:, 0:1024], woSl[0][:, 0:1024],
                 reads=["wqSl0", "wqSl1"], writes=["tapo"], tag="tap", force_inc=True)
            return
        if tap is not None and tap[0] in ("qa1", "ka1", "va1", "qa0", "ka0", "va0"):
            tsr = {"qa1": (q_aug[1], "qga1"), "ka1": (k_aug[1], "kga1"),
                   "va1": (v_aug[1], "vga1"), "qa0": (q_aug[0], "qga0"),
                   "ka0": (k_aug[0], "kga0"), "va0": (v_aug[0], "vga0")}[tap[0]]
            dmaY(tap_t.ap()[0:65, :], tsr[0][0:65, 0:1088] if tap[0].startswith("va")
                 else tsr[0][:, 0:2048],
                 reads=[tsr[1]], writes=["tapo"], tag="tap", force_inc=True)
            return
        if tap is not None and tap[0] == "aout":
            tap_sbuf("aout", r2_bf, ["aout"], 16384)
            return
        P.claim(["woSl0", "woSl1"], ["wqSl0", "wqSl1"])
        for oc in range(8):
            w = woSl[0]
            wk2 = "woSl0"
            dmaG(bass.AP(w, 0, [[1024, 128], [128, 8], [1, 128]]),
                 bass.AP(W["aWo"], 128 * oc, [[1024, 128], [131072, 8], [1, 128]]),
                 reads=[], writes=[wk2], tag=f"ldWo{oc}")
            for fc in range(8):
                mm(4, pb(4, cn=1), w[:, 128 * fc:128 * fc + 128], bvc[:, fc:fc + 1],
                   first=(fc == 0), last=(fc == 7), reads=[wk2, "bvc"], tag=f"bop{oc}")
            vop(lambda e, oc=oc: e.tensor_add(bopc[:, oc:oc + 1], pb(4, cn=1),
                                              boc[:, oc:oc + 1]),
                reads=["pb4", "boc"], writes=["bopc"], tag=f"bope{oc}", shortw=True)
            for ts in range(4):
                b = ts % 4
                for fc in range(8):
                    mm(b, pb(b), w[:, 128 * fc:128 * fc + 128],
                       r2_bf[:, 2048 * fc + 512 * ts:2048 * fc + 512 * ts + 512],
                       first=(fc == 0), last=(fc == 7), reads=[wk2, "aout"],
                       tag=f"wo{oc}")
                vop(lambda e, o=XC(oc)[:, 512 * ts:512 * ts + 512], i=pb(b), oc=oc:
                    e.scalar_tensor_tensor(o, i, bopc[:, oc:oc + 1], o, OP.add, OP.add),
                    reads=[f"pb{b}", "bopc", f"x{oc}"], writes=[f"x{oc}"], tag=f"woe{oc}")

    def ffn(fid):
        P.claim(["hact"], ["xbf", "lntmp0", "lntmp1", "lntmp2", "lntmp3"])
        P.claim(["fw10", "fw11", "fw20", "fw21"],
                ["wqSl0", "wqSl1", "woSl0", "woSl1", "aWkd_s", "aWku_s",
                 "bWk_s", "vtm"])
        for tt_ in range(4):
            tb = 512 * tt_
            for ic in range(32):
                w = fW1b[ic % 2]
                wk2 = f"fw1{ic % 2}"
                dmaG(bass.AP(w, 0, [[1024, 128], [128, 8], [1, 128]]),
                     bass.AP(W["fW1"], 128 * ic, [[4096, 128], [524288, 8], [1, 128]]),
                     reads=[], writes=[wk2], tag=f"ldW1_{fid}")
                b = ic % 2
                for hc in range(8):
                    mm(b, pb(b), w[:, 128 * hc:128 * hc + 128],
                       XNC(hc)[:, tb:tb + 512], first=(hc == 0), last=(hc == 7),
                       reads=[wk2, f"xn{hc}"], tag=f"f1_{fid}")
                sop(lambda e, o=r2_bf[:, 512 * ic:512 * ic + 512], i=pb(b), ic=ic:
                    e.activation(o, i, AF.Relu, bias=b1c[:, ic:ic + 1], scale=1.0),
                    reads=[f"pb{b}", "b1c"], writes=["hact"], tag=f"f1e_{fid}")
            for oc in range(8):
                w2 = fW2b[oc % 2]
                wk2 = f"fw2{oc % 2}"
                dmaG(bass.AP(w2, 0, [[4096, 128], [128, 32], [1, 128]]),
                     bass.AP(W["fW2"], 128 * oc, [[1024, 128], [131072, 32], [1, 128]]),
                     reads=[], writes=[wk2], tag=f"ldW2_{fid}")
                b = 2 + oc % 2
                for ic in range(32):
                    mm(b, pb(b), w2[:, 128 * ic:128 * ic + 128],
                       r2_bf[:, 512 * ic:512 * ic + 512],
                       first=(ic == 0), last=(ic == 31), reads=[wk2, "hact"],
                       tag=f"f2_{fid}")
                vop(lambda e, o=XC(oc)[:, tb:tb + 512], i=pb(b), oc=oc:
                    e.scalar_tensor_tensor(o, i, b2c[:, oc:oc + 1], o, OP.add, OP.add),
                    reads=[f"pb{b}", "b2c", f"x{oc}"], writes=[f"x{oc}"],
                    tag=f"f2e_{fid}")

    def attn_b():
        P.claim(["bWk_s"], ["fw10", "fw11", "fw20", "fw21"])
        P.claim(["A_t2"], ["C_t", "S_t", "E_t"])
        P.claim(["Gt"], ["kvd"])
        P.claim(["raug"], ["kvd"])
        P.claim(["qaug2"], ["vga0", "vga1", "attE0", "attE1", "attE2"])
        P.claim(["wvStg", "wvT2a", "wvT2b", "bqSl0", "bqSl1"],
                ["repA", "repB", "qga0", "qga1", "kga0", "kga1", "scr"])
        P.claim(["qprime"], ["xbf", "lntmp0", "lntmp1", "lntmp2", "lntmp3"])
        P.claim(["stb0", "stb1"], ["sgl", "nzt"])
        dmaG(Gt_s[:, :], CN["cG"].ap()[:, :], reads=[], writes=["Gt"], tag="ldG")
        dmaG(bass.AP(bWk_s, 0, [[8192, 128], [1024, 8], [1, 1024]]),
             bass.AP(W["bWk"], 0, [[1024, 128], [131072, 8], [1, 1024]]),
             reads=[], writes=["bWk_s"], tag="ldbWk")
        for h1c in range(8):
            w = bqSl[h1c % 2]
            wk2 = f"bqSl{h1c % 2}"
            dmaG(bass.AP(w, 0, [[1024, 128], [128, 8], [1, 128]]),
                 bass.AP(W["bWq"], 128 * h1c, [[1024, 128], [131072, 8], [1, 128]]),
                 reads=[], writes=[wk2], tag=f"ldbWq{h1c}")
            for half in range(2):
                b = (h1c % 2) * 2 + half
                for oc in range(8):
                    mm(b, pb(b), w[:, 128 * oc:128 * oc + 128],
                       bWk_s[:, 1024 * oc + 512 * half:1024 * oc + 512 * half + 512],
                       first=(oc == 0), last=(oc == 7), reads=[wk2, "bWk_s"],
                       tag=f"A{h1c}")
                vop(lambda e, o=A_s[:, 1024 * h1c + 512 * half:1024 * h1c + 512 * half + 512],
                    i=pb(b): e.tensor_copy(o, i),
                    reads=[f"pb{b}"], writes=["A_t2"], tag=f"Ae{h1c}")
            for oc in range(8):
                mm(4, pb(4, cn=1), w[:, 128 * oc:128 * oc + 128], bkbc[:, oc:oc + 1],
                   first=(oc == 0), last=(oc == 7), reads=[wk2, "bkbc"], tag=f"u{h1c}")
            vop(lambda e, h1c=h1c: e.tensor_copy(ucol[:, h1c:h1c + 1], pb(4, cn=1)),
                reads=["pb4"], writes=["ucol"], tag=f"ue{h1c}", shortw=True)
        for h2c in range(8):
            for oc in range(8):
                mm(5, pb(5, cn=1), bWk_s[:, 1024 * oc + 128 * h2c:1024 * oc + 128 * h2c + 128],
                   bqbc[:, oc:oc + 1], first=(oc == 0), last=(oc == 7),
                   reads=["bWk_s", "bqbc"], tag=f"w{h2c}")
            vop(lambda e, h2c=h2c: e.tensor_copy(wcol[:, h2c:h2c + 1], pb(5, cn=1)),
                reads=["pb5"], writes=["wcol"], tag=f"we{h2c}", shortw=True)
        for oc in range(8):
            mm(6, pb(6, p1=1, cn=1), bqbc[:, oc:oc + 1], bkbc[:, oc:oc + 1],
               first=(oc == 0), last=(oc == 7), reads=["bqbc", "bkbc"], tag="s0")
        vop(lambda e: e.tensor_copy(s0c[0:1, :], pb(6, p1=1, cn=1)),
            reads=["pb6"], writes=["s0c"], tag="s0e", shortw=True)

        for gc in range(8):
            for ts in range(4):
                b = ts % 4
                for hc in range(8):
                    mm(b, pb(b), A_s[:, 1024 * hc + 128 * gc:1024 * hc + 128 * gc + 128],
                       XNC(hc)[:, 512 * ts:512 * ts + 512],
                       first=(hc == 0), last=(hc == 7), reads=["A_t2", f"xn{hc}"],
                       tag=f"qp{gc}")
                sop(lambda e, o=r2_bf[:, 2048 * gc + 512 * ts:2048 * gc + 512 * ts + 512],
                    i=pb(b): e.activation(o, i, AF.Identity, bias=zeroc[:, :], scale=1.0),
                    reads=[f"pb{b}", "zeroc"], writes=["qprime"], tag=f"qpe{gc}")
        for nm, col, colk in [("rq", ucol, "ucol"), ("rk", wcol, "wcol")]:
            for s in range(4):
                b = 4 + s
                for hc in range(8):
                    mm(b, ps[0:1, 512 * b:512 * b + 512], col[:, hc:hc + 1],
                       XNC(hc)[:, 512 * s:512 * s + 512],
                       first=(hc == 0), last=(hc == 7), reads=[colk, f"xn{hc}"],
                       tag=nm)
            if nm == "rq":
                P.op("G", lambda e: e.memset(qaug2[0:2, :], 1.0), writes=["qaug2"],
                     tag="rqo")
                vop(lambda e: e.tensor_scalar(qaug2[0:1, :], ps[0:1, 2048:4096],
                                              s0c[0:1, :], None, OP.add),
                    reads=["pb4", "pb5", "pb6", "pb7", "s0c", "qaug2"],
                    writes=["qaug2"], tag="rqr")
            else:
                vop(lambda e: e.tensor_copy(raug[0:1, :], ps[0:1, 2048:4096]),
                    reads=["pb4", "pb5", "pb6", "pb7"], writes=["raug"], tag="rkc")
                dmaG(raug[1:2, :], raug[0:1, :], reads=["raug"], writes=["raug"],
                     tag="rkm")
                P.op("G", lambda e: e.memset(raug[0:1, :], 1.0), writes=["raug"],
                     tag="rko")
        P.claim(["vtm"], ["bWk_s", "A_t2", "fw10", "fw11", "fw20", "fw21", "onesel", "km20", "km21", "km2s", "mkf0", "mkf1", "mk0", "mk1"])
        for oc in range(8):
            dmaY(wvStg[:, :], W["bWv"].ap()[128 * oc:128 * oc + 128, :],
                 reads=["wvStg"], writes=["wvStg"], tag=f"ldWv{oc}")
            for hc in range(8):
                wsl = hc % 2
                P.op("T", lambda e, o=pb(3, cn=128), i=wvStg[:, 128 * hc:128 * hc + 128]:
                     e.transpose(o, i, ident[:, :]),
                     reads=["wvStg", "I_t"], writes=["pb3"], tag=f"wvT{oc}")
                vop(lambda e, o=wvT2[wsl][:, :], i=pb(3, cn=128): e.tensor_copy(o, i),
                    reads=["pb3"], writes=[f"wvT2{wsl}"], tag=f"wvTe{oc}")
                for tt_ in range(16):
                    b = 4 + tt_ // 4
                    mm(b, pb(b, c0=128 * (tt_ % 4), cn=128),
                       XNC(hc)[:, 128 * tt_:128 * tt_ + 128], wvT2[wsl][:, :],
                       first=(hc == 0 and tt_ % 4 == 0),
                       last=(hc == 7 and tt_ % 4 == 3),
                       reads=[f"xn{hc}", f"wvT2{wsl}"], tag=f"vt{oc}")
            for tt_ in range(16):
                b = 4 + tt_ // 4
                vop(lambda e, o=vtm[:, 1024 * tt_ + 128 * oc:1024 * tt_ + 128 * oc + 128],
                    i=pb(b, c0=128 * (tt_ % 4), cn=128), oc=oc: e.tensor_copy(o, i),
                    reads=[f"pb{b}"], writes=["vtm"], tag=f"vte{oc}")
        for s in range(8):
            q0 = 256 * s
            last_kb = 2 * s + 1
            for kb in range(last_kb + 1):
                sb = kb % 2
                slot = kb % 2
                for hc in range(8):
                    mm(sb, pb(sb, cn=256), XNC(hc)[:, 128 * kb:128 * kb + 128],
                       r2_bf[:, 2048 * hc + q0:2048 * hc + q0 + 256],
                       first=(hc == 0), last=False, reads=[f"xn{hc}", "qprime"],
                       tag=f"bs{s}")
                mm(sb, pb(sb, cn=256), raug[:, 128 * kb:128 * kb + 128],
                   qaug2[:, q0:q0 + 256], first=False, last=True,
                   reads=["raug", "qaug2"], tag=f"bs2{s}")
                d0 = 2 * s - kb + 1
                vop(lambda e, o=sglb[:, 256 * slot:256 * slot + 256], i=pb(sb, cn=256),
                    g=Gt_s[:, 128 * d0:128 * d0 + 256]: e.tensor_mul(o, i, g),
                    reads=[f"pb{sb}", "Gt"], writes=[f"stb{slot}"], tag=f"bm{s}")
                for hcb in range(8):
                    ob = 4 + hcb // 2
                    mm(ob, pb(ob, c0=256 * (hcb % 2), cn=256),
                       vtm[:, 1024 * kb + 128 * hcb:1024 * kb + 128 * hcb + 128],
                       sglb[:, 256 * slot:256 * slot + 256],
                       first=(kb == 0 and hcb % 2 == 0),
                       last=(kb == last_kb and hcb % 2 == 1),
                       reads=["vtm", f"stb{slot}"], tag=f"bv{s}")
            for hcb in range(8):
                ob = 4 + hcb // 2
                vop(lambda e, o=XC(hcb)[:, q0:q0 + 256],
                    i=pb(ob, c0=256 * (hcb % 2), cn=256), hcb=hcb:
                    e.tensor_add(o, i, o),
                    reads=[f"pb{ob}", f"x{hcb}"], writes=[f"x{hcb}"], tag=f"be{s}")

    def exit_phase():
        P.claim(["xt0", "xt1"], ["xbf", "lntmp0", "lntmp1", "lntmp2", "lntmp3",
                                 "hact", "qprime", "aout"])
        for tt_ in range(16):
            sl = tt_ % 2
            osb = xt_v[sl]
            for fc in range(8):
                b = fc % 4
                P.op("T", lambda e, o=pb(b, cn=128), i=XC(fc)[:, 128 * tt_:128 * tt_ + 128]:
                     e.transpose(o, i, ident[:, :]),
                     reads=[f"x{fc}", "I_t"], writes=[f"pb{b}"], tag="oT")
                vop(lambda e, o=osb[:, 128 * fc:128 * fc + 128], i=pb(b, cn=128):
                    e.tensor_copy(o, i),
                    reads=[f"pb{b}"], writes=[f"xt{sl}"], tag="oTe")
            dmaY(out_t.ap()[128 * tt_:128 * tt_ + 128, :], osb,
                 reads=[f"xt{sl}"], writes=[f"xt{sl}"], tag="out", force_inc=True)

    layernorm(1)
    done = tap_xn("xn1")
    if not done and n_phases >= 1:
        attn_a()
        done = tap_x("x1")
    if not done and n_phases >= 2:
        layernorm(2)
        ffn(1)
        done = tap_x("x2")
    if not done and n_phases >= 3:
        layernorm(3)
        attn_b()
        done = tap_x("x3")
    if not done and n_phases >= 4:
        layernorm(4)
        ffn(2)
        exit_phase()

    P.finalize()
    from contextlib import ExitStack
    ctx = ExitStack()
    keys = set(P.final_counts.keys()) | set(_ENGS)
    sems = {k: ctx.enter_context(nc.semaphore("sm_" + k.replace(":", "_")))
            for k in sorted(keys)}
    block = ctx.enter_context(nc.Block())
    P.emit(nc, block, sems)
    ctx.close()
    return nc

def kernel(**inputs):
    cons = _consts()
    nc = _build()
    x = np.ascontiguousarray(inputs["L0_attn_a_x"], dtype=np.float32)
    base = {
        "aWq": inputs["L0_attn_a_Wq"], "abq": inputs["L0_attn_a_bq"],
        "aWkd": inputs["L0_attn_a_Wkv_down"], "abkd": inputs["L0_attn_a_bkv_down"],
        "aWku": inputs["L0_attn_a_Wkv_up"], "abku": inputs["L0_attn_a_bkv_up"],
        "aWo": inputs["L0_attn_a_Wo"], "abo": inputs["L0_attn_a_bo"],
        "bWq": inputs["L0_attn_b_Wq"], "bbq": inputs["L0_attn_b_bq"],
        "bWk": inputs["L0_attn_b_Wk"], "bbk": inputs["L0_attn_b_bk"],
        "bWv": inputs["L0_attn_b_Wv"],
        "fW1": inputs["L0_ffn_W1"], "fb1": inputs["L0_ffn_b1"],
        "fW2": inputs["L0_ffn_W2"], "fb2": inputs["L0_ffn_b2"],
    }
    base = {k: np.ascontiguousarray(v, dtype=np.float32) for k, v in base.items()}
    base.update(cons)
    in_maps = []
    for c in range(NC_):
        m = dict(base)
        m["x"] = np.ascontiguousarray(x[c])
        in_maps.append(m)
    res = run_bass_kernel_spmd(nc, in_maps, core_ids=list(range(NC_)))
    return np.stack([r["out"] for r in res.results], axis=0)


# version 03 (final)
# speedup vs baseline: 1.4031x; 1.4031x over previous
import numpy as np
import ml_dtypes
import concourse.bass as bass
import concourse.mybir as mybir
from concourse.bass_utils import run_bass_kernel_spmd

F32 = mybir.dt.float32
BF16 = mybir.dt.bfloat16
AF = mybir.ActivationFunctionType
OP = mybir.AluOpType
AXX = mybir.AxisListType.X

T, D, NH, HD, DL, DI = 2048, 1024, 16, 64, 256, 4096
_DEV_NHEADS = None
_DEV_SERIAL = False
_DEV_RACECHK = False
_DEV_NOE = False
_DEV_NONORM = False
_DEV_TCUT = 99
_DEV_NOTAIL = False
_DEV_B1 = False
_DEV_QB2P0 = False
NC_ = 8

def _consts():
    bf = ml_dtypes.bfloat16
    t = np.arange(T, dtype=np.float64)
    inv = 10000.0 ** (-np.arange(0, HD, 2, dtype=np.float64) / HD)
    ang = np.outer(inv, t)
    c32, s32 = np.cos(ang), np.sin(ang)
    C = np.zeros((128, T), dtype=np.float64)
    S = np.zeros((128, T), dtype=np.float64)
    for blk in range(4):
        r = 32 * blk
        C[r:r + 32] = c32
        S[r:r + 32] = -s32 if blk % 2 == 0 else s32
    kk = np.arange(128)[:, None]
    E = np.zeros((128, 31 * 128), dtype=np.float64)
    for dlt in range(31):
        qq = np.arange(128)[None, :]
        d = 128 * (dlt - 15) + qq - kk
        f = -0.1 * np.log(np.abs(d) + 1.0) + np.log(np.exp(-np.maximum(d, 0) * 0.01) + 1e-9)
        E[:, 128 * dlt:128 * dlt + 128] = np.exp(f)
    GAM = 0.96875
    G = np.zeros((128, 17 * 128), dtype=np.float64)
    for dlt in range(-1, 16):
        qq = np.arange(128)[None, :]
        e = 128 * dlt + qq - kk
        G[:, 128 * (dlt + 1):128 * (dlt + 2)] = np.where(e >= 0, GAM ** e.astype(np.float64), 0.0)
    P = np.zeros((128, 128), dtype=np.float32)
    for m in range(128):
        p = m + 32 if (m % 64) < 32 else m - 32
        P[p, m] = 1.0
    I = np.eye(128, dtype=np.float32)
    def as_bf_f32(a):
        return a.astype(bf).astype(np.float32)
    return {
        "cC": as_bf_f32(C), "cS": as_bf_f32(S),
        "cE": as_bf_f32(E), "cG": as_bf_f32(G),
        "cP": P.astype(np.float32), "cI": I,
    }

_ENGS = ("T", "V", "S", "Y", "G")

class _Plan:
    def __init__(self):
        self.ops = []
        self.res = {}

    def op(self, eng, emit, reads=(), writes=(), dma=False, tag="",
           force_inc=False, shortw=False):
        opid = len(self.ops)
        deps = set()
        for r in reads:
            st = self.res.get(r)
            if st is not None and st["w"] is not None:
                deps.add(st["w"])
        for r in writes:
            st = self.res.get(r)
            if st is not None:
                if st["w"] is not None:
                    deps.add(st["w"])
                deps.update(st["r"])
        semkey = ("dma:" + writes[0]) if dma else eng
        if _DEV_SERIAL and opid > 0:
            deps.add(opid - 1)
        o = dict(id=opid, eng=eng, emit=emit, deps=deps, dma=dma,
                 need=force_inc or dma, val=None, tag=tag, semkey=semkey,
                 shortw=shortw)
        self.ops.append(o)
        for r in reads:
            self.res.setdefault(r, {"w": None, "r": []})["r"].append(opid)
        for r in writes:
            self.res[r] = {"w": opid, "r": []}
        return opid

    def claim(self, news, olds):
        pend = []
        for o in olds:
            st = self.res.get(o)
            if st is None:
                continue
            if st["w"] is not None:
                pend.append(st["w"])
            pend.extend(st["r"])
        for n in news:
            st = self.res.setdefault(n, {"w": None, "r": []})
            st["r"] = list(set(st["r"]) | set(pend))

    def finalize(self):
        for o in self.ops:
            for d in o["deps"]:
                p = self.ops[d]
                if p["eng"] != o["eng"] or p["dma"] or p["shortw"]:
                    p["need"] = True
        cnt = {}
        for o in self.ops:
            if o["need"]:
                k = o["semkey"]
                cnt[k] = cnt.get(k, 0) + (16 if o["dma"] else 1)
                o["val"] = cnt[k]
        self.final_counts = cnt

    def emit(self, nc, block, sems):
        plan = self
        def eng_fn(engname):
            def fn(eng):
                waited = {}
                for o in plan.ops:
                    if o["eng"] != engname:
                        continue
                    waits = {}
                    for d in o["deps"]:
                        p = plan.ops[d]
                        if p["eng"] == engname and not p["dma"] and not p["shortw"]:
                            continue
                        assert p["val"] is not None, (o["tag"], p["tag"])
                        k = p["semkey"]
                        waits[k] = max(waits.get(k, 0), p["val"])
                    for pe, v in sorted(waits.items()):
                        if waited.get(pe, 0) >= v:
                            continue
                        eng.wait_ge(sems[pe], v)
                        waited[pe] = v
                    inst = o["emit"](eng)
                    if o["need"]:
                        assert inst is not None, o["tag"]
                        inst.then_inc(sems[o["semkey"]], 16 if o["dma"] else 1)
                if engname == "Y":
                    for k, v in sorted(plan.final_counts.items()):
                        if k in ("dma:out", "dma:tapo"):
                            eng.wait_ge(sems[k], v)
            return fn
        block.tensor(lambda t: eng_fn("T")(t))
        block.vector(lambda v: eng_fn("V")(v))
        block.scalar(lambda s: eng_fn("S")(s))
        block.sync(lambda y: eng_fn("Y")(y))
        block.gpsimd(lambda g: eng_fn("G")(g))

class _BankGroups:
    def __init__(self):
        self.open = set()
    def flags(self, bank, last):
        start = bank not in self.open
        if start:
            self.open.add(bank)
        if last:
            self.open.discard(bank)
        return start, last

def _build(n_phases=99, tap=None, salt=None):
    nc = bass.Bass("TRN2", target_bir_lowering=False,
                   detect_race_conditions=_DEV_RACECHK)

    x_in = nc.dram_tensor("x", [T, D], F32, kind="ExternalInput")
    W = {}
    for nm, shp in [
        ("aWq", [D, D]), ("abq", [D]), ("aWkd", [D, DL]), ("abkd", [DL]),
        ("aWku", [DL, 2 * D]), ("abku", [2 * D]), ("aWo", [D, D]), ("abo", [D]),
        ("bWq", [D, D]), ("bbq", [D]), ("bWk", [D, D]), ("bbk", [D]),
        ("bWv", [D, D]),
        ("fW1", [D, DI]), ("fb1", [DI]), ("fW2", [DI, D]), ("fb2", [D]),
    ]:
        W[nm] = nc.dram_tensor(nm, shp, F32, kind="ExternalInput")
    CN = {}
    for nm, shp in [("cC", [128, T]), ("cS", [128, T]), ("cE", [128, 31 * 128]),
                    ("cG", [128, 17 * 128]), ("cP", [128, 128]), ("cI", [128, 128])]:
        CN[nm] = nc.dram_tensor(nm, shp, F32, kind="ExternalInput")
    out_t = nc.dram_tensor("out", [T, D], F32, kind="ExternalOutput")
    tap_t = None
    if tap is not None:
        tap_t = nc.dram_tensor("tapout", list(tap[1]), tap[2], kind="ExternalOutput")

    SB0 = 16384
    def at(name, shape, dtype, offset):
        return nc.alloc_sbuf_tensor_at(name, shape, dtype, offset=SB0 + offset)
    x_fm   = at("x_fm", [128, 8 * T], F32, 0)
    xn     = at("xn", [128, 8 * T], BF16, 65536)
    r2_bf  = at("r2_bf", [128, 8 * T], BF16, 98304)
    r2_f32 = at("r2_f32", [128, 4 * T], F32, 98304)
    _wo0   = at("woSl0", [128, D], BF16, 131072)
    woSl   = [_wo0, _wo0]
    bvr    = at("bvr", [1, 1024], BF16, 133120)
    aWkd_s = at("aWkd_s", [128, 2048], BF16, 135168)
    aWku_s = at("aWku_s", [128, 4096], BF16, 139264)
    C_s    = at("C_s", [128, T], BF16, 147456)
    S_s    = at("S_s", [128, T], BF16, 151552)
    E_s    = at("E_s", [128, 31 * 128], BF16, 155648)
    onesel = at("onesel", [65, 64], BF16, 163584)
    kmax2  = at("kmax2", [128, 4], F32, 163712)
    mkf    = at("mkf", [64, 2], F32, 163744)
    mkcol  = at("mkcol", [64, 2], BF16, 163776)
    fW1b   = [at("fW1a", [128, D], BF16, 131072),
              at("fW1b", [128, D], BF16, 133120)]
    fW2b   = [at("fW2a", [128, 4096], BF16, 135168),
              at("fW2b", [128, 4096], BF16, 143360)]
    bWk_s  = at("bWk_s", [128, 8 * D], BF16, 131072)
    A_s    = at("A_s", [128, 8 * D], BF16, 147456)
    vtm    = at("vtm", [128, 16 * D], BF16, 131072)
    repA   = at("repA", [128, T], F32, 163840)
    repB   = at("repB", [128, T], F32, 172032)
    q_aug  = [at("q_aug0", [65, T], BF16, 163840),
              at("q_aug1", [65, T], BF16, 167936)]
    k_aug  = [at("k_aug0", [65, T], BF16, 172032),
              at("k_aug1", [65, T], BF16, 176128)]
    scr    = at("scr", [64, T], BF16, 180224)
    wvStg  = at("wvStg", [128, D], F32, 163840)
    wvT2   = [at("wvT2a", [128, 128], BF16, 167936),
              at("wvT2b", [128, 128], BF16, 168192)]
    bqSl   = [at("bqSl0", [128, D], BF16, 168448),
              at("bqSl1", [128, D], BF16, 170496)]
    kvd_s  = at("kvd_s", [128, 2 * T], BF16, 184320)
    Gt_s   = at("Gt_s", [128, 17 * 128], BF16, 184320)
    raug   = at("raug", [2, T], BF16, 188672)
    v_aug  = [at("v_aug0", [128, 16 * 68], BF16, 193024),
              at("v_aug1", [128, 16 * 68], BF16, 195200)]
    attE   = [at("attE0", [128, 512], BF16, 197376),
              at("attE1", [128, 512], BF16, 198400),
              at("attE2", [128, 512], BF16, 199424)]
    qaug2  = at("qaug2", [2, T], BF16, 193024)
    _al = lambda n: (n + 31) & ~31
    m0 = 200448
    def atm(name, shape, dtype, size):
        nonlocal m0
        h = at(name, shape, dtype, m0)
        m0 += _al(size)
        return h
    rowa   = atm("rowa", [128, 1024], F32, 4096)
    qscr   = at("qscr", [64, 2048], BF16, 200448)
    sglb_off = m0
    sglb   = atm("sglb", [128, 1024], BF16, 2048)
    sglb2  = atm("sglb2", [128, 1024], BF16, 2048)
    repsb  = atm("repsb", [64, 512], BF16, 1024)
    nztmp  = atm("nztmp", [64, 512], BF16, 1024)
    ident  = atm("ident", [128, 128], F32, 512)
    prot   = atm("prot", [64, 64], BF16, 128)
    ones_b = atm("ones_b", [128, 128], BF16, 256)
    ones_f = atm("ones_f", [128, 128], F32, 512)
    bqc64  = atm("bqc64", [64, 16], F32, 64)
    bkuk   = atm("bkuk", [64, 16], F32, 64)
    bkdc   = atm("bkdc", [128, 2], F32, 8)
    bvc    = atm("bvc", [128, 8], BF16, 16)
    boc    = atm("boc", [128, 8], F32, 32)
    bopc   = atm("bopc", [128, 8], F32, 32)
    b1c    = atm("b1c", [128, 32], F32, 128)
    b2c    = atm("b2c", [128, 8], F32, 32)
    bqbc   = atm("bqbc", [128, 8], BF16, 16)
    bkbc   = atm("bkbc", [128, 8], BF16, 16)
    ucol   = atm("ucol", [128, 8], BF16, 16)
    wcol   = atm("wcol", [128, 8], BF16, 16)
    mvrow  = atm("mvrow", [128, 136], BF16, 272)




    cst    = atm("cst", [128, 8], F32, 32)
    epsc, tinyc, onec, safc, s0c, zeroc, ln02c = (cst[:, i:i + 1] for i in range(7))
    assert SB0 + m0 <= 229344, m0

    ps = nc.alloc_psum_tensor("ps", [128, 4096], F32)
    def pb(i, p0=0, p1=128, c0=0, cn=512):
        assert c0 + cn <= 512, (c0, cn)
        return ps[p0:p1, 512 * i + c0:512 * i + c0 + cn]

    P = _Plan()
    BG = _BankGroups()
    if salt is not None:
        P.op("G", lambda e, v=float(salt): e.memset(rowa[0:1, 768:1024], v),
             writes=["salt"], tag="salt", force_inc=True)

    def mm(bank, out_ap, lhsT, rhs, first, last, reads, tile_position=None,
           tag="", dup=False):
        assert not dup or (first and last)
        for _ in range(2 if dup else 1):
            start, stop = BG.flags(bank, last)
            def em(e, o=out_ap, l=lhsT, r=rhs, st=start, sp=stop, tp=tile_position):
                return e.matmul(o, l, r, start=st, stop=sp, tile_position=tp)
            P.op("T", em, reads=list(reads), writes=[f"pb{bank}"], tag=tag or "mm")

    def dmaY(out_ap, in_ap, reads, writes, tag="", force_inc=False, slow=False):
        P.op("Y", lambda e, o=out_ap, i=in_ap, sl=slow:
             e.dma_start(out=o, in_=i, allow_slow_non_contiguous=sl),
             reads=list(reads), writes=list(writes), dma=True, tag=tag or "dmaY",
             force_inc=force_inc)

    def dmaG(out_ap, in_ap, reads, writes, tag="", force_inc=False, slow=False):
        P.op("G", lambda e, o=out_ap, i=in_ap, sl=slow:
             e.dma_start(out=o, in_=i, allow_slow_non_contiguous=sl),
             reads=list(reads), writes=list(writes), dma=True, tag=tag or "dmaG",
             force_inc=force_inc)

    def vop(emit, reads, writes, tag="", dup=False, shortw=False):
        for _ in range(2 if dup else 1):
            P.op("V", emit, reads=list(reads), writes=list(writes),
                 tag=tag or "v", shortw=shortw)

    def sop(emit, reads, writes, tag="", dup=False, shortw=False):
        for _ in range(2 if dup else 1):
            P.op("S", emit, reads=list(reads), writes=list(writes),
                 tag=tag or "s", shortw=shortw)

    XC = lambda c: x_fm[:, 2048 * c:2048 * c + 2048]
    XNC = lambda c: xn[:, 2048 * c:2048 * c + 2048]
    R2B = lambda c: r2_bf[:, 2048 * c:2048 * c + 2048]
    ALLX = [f"x{c}" for c in range(8)]
    ALLXN = [f"xn{c}" for c in range(8)]

    for nm, dst in [("cC", C_s), ("cS", S_s)]:
        dmaG(dst[0:64, :], CN[nm].ap()[0:64, :], reads=[], writes=[nm[1:] + "_t"],
             tag="ld" + nm)
    dmaG(E_s[:, :], CN["cE"].ap()[:, :], reads=[], writes=["E_t"], tag="ldcE")
    dmaG(prot[:, :], CN["cP"].ap()[0:64, 0:64], reads=[], writes=["P_t"], tag="ldcP")
    dmaY(ident[:, :], CN["cI"].ap()[:, :], reads=[], writes=["I_t"], tag="ldcI")
    P.op("G", lambda e: e.memset(ones_b[:, :], 1.0), writes=["ones_b"])
    P.op("G", lambda e: e.memset(ones_f[:, :], 1.0), writes=["ones_f"])
    P.op("G", lambda e: e.memset(epsc[:, :], 1e-5), writes=["epsc"])
    P.op("G", lambda e: e.memset(tinyc[:, :], 1e-35), writes=["tinyc"])
    P.op("G", lambda e: e.memset(onec[:, :], 1.0), writes=["onec"])
    P.op("G", lambda e: e.memset(safc[:, :], 0.1), writes=["safc"])
    P.op("G", lambda e: e.memset(zeroc[:, :], 0.0), writes=["zeroc"])
    P.op("G", lambda e: e.memset(onesel[:, :], 0.0), writes=["onesel"])
    P.op("G", lambda e: e.memset(onesel[64:65, :], 1.0), writes=["onesel"])
    P.op("G", lambda e: e.memset(sglb[:, :], 0.0), writes=["sgl", "repsb", "nzt",
                                                           "stb0", "stb1"])
    P.op("G", lambda e: e.memset(sglb2[:, :], 0.0), writes=["sgl2"])
    P.op("G", lambda e: e.memset(ln02c[:, :], -1.6094379124341003), writes=["ln02c"])

    dmaY(bass.AP(bqc64, 0, [[16, 64], [1, 16]]),
         bass.AP(W["abq"], 0, [[1, 64], [64, 16]]), reads=[], writes=["bqc64"],
         tag="ldbq", slow=True)
    vop(lambda e: e.tensor_scalar_mul(bqc64[:, :], bqc64[:, :], 0.125),
        reads=["bqc64"], writes=["bqc64"], tag="sclbq", shortw=True)
    P.op("G", lambda e: e.memset(bkuk[:, :], 0.0), writes=["bkuk"])
    dmaY(bass.AP(bkuk, 0, [[16, 64], [1, 16]]),
         bass.AP(W["abku"], 0, [[1, 64], [128, 16]]), reads=["bkuk"], writes=["bkuk"],
         tag="ldbkuk", slow=True)
    dmaY(bass.AP(bkdc, 0, [[2, 128], [1, 2]]),
         bass.AP(W["abkd"], 0, [[1, 128], [128, 2]]), reads=[], writes=["bkdc"],
         tag="ldbkd", slow=True)
    for g in range(2):
        dmaG(bass.AP(bvc, 8 * 64 * g, [[8, 64], [1, 8]]),
             bass.AP(W["abku"], 64 + 128 * g, [[1, 64], [256, 8]]),
             reads=["bvc"], writes=["bvc"], tag="ldbv", slow=True)
    dmaG(bass.AP(bvr, 0, [[1024, 1], [64, 16], [1, 64]]),
         bass.AP(W["abku"], 64, [[1, 1], [128, 16], [1, 64]]),
         reads=[], writes=["bvr"], tag="ldbvr", slow=True)
    dmaY(bass.AP(boc, 0, [[8, 128], [1, 8]]),
         bass.AP(W["abo"], 0, [[1, 128], [128, 8]]), reads=[], writes=["boc"],
         tag="ldbo", slow=True)
    dmaY(bass.AP(b1c, 0, [[32, 128], [1, 32]]),
         bass.AP(W["fb1"], 0, [[1, 128], [128, 32]]), reads=[], writes=["b1c"],
         tag="ldb1", slow=True)
    dmaY(bass.AP(b2c, 0, [[8, 128], [1, 8]]),
         bass.AP(W["fb2"], 0, [[1, 128], [128, 8]]), reads=[], writes=["b2c"],
         tag="ldb2", slow=True)
    dmaG(bass.AP(bqbc, 0, [[8, 128], [1, 8]]),
         bass.AP(W["bbq"], 0, [[1, 128], [128, 8]]), reads=[], writes=["bqbc"],
         tag="ldbqb", slow=True)
    dmaG(bass.AP(bkbc, 0, [[8, 128], [1, 8]]),
         bass.AP(W["bbk"], 0, [[1, 128], [128, 8]]), reads=[], writes=["bkbc"],
         tag="ldbkb", slow=True)

    dmaG(bass.AP(aWkd_s, 0, [[2048, 128], [256, 8], [1, 256]]),
         bass.AP(W["aWkd"], 0, [[256, 128], [32768, 8], [1, 256]]),
         reads=[], writes=["aWkd_s"], tag="ldaWkd")
    dmaG(bass.AP(aWku_s, 0, [[4096, 128], [2048, 2], [1, 2048]]),
         bass.AP(W["aWku"], 0, [[2048, 128], [262144, 2], [1, 2048]]),
         reads=[], writes=["aWku_s"], tag="ldaWku")

    xt_v = [r2_f32[:, 0:1024], r2_f32[:, 1024:2048]]
    for tt in range(16):
        sl = tt % 2
        xt = xt_v[sl]
        dmaY(xt, x_in.ap()[128 * tt:128 * tt + 128, :],
             reads=[], writes=[f"xt{sl}"], tag=f"ldx{tt}")
        for fc in range(8):
            b = fc % 4
            P.op("T", lambda e, o=pb(b, cn=128), i=xt[:, 128 * fc:128 * fc + 128]:
                 e.transpose(o, i, ident[:, :]),
                 reads=[f"xt{sl}", "I_t"], writes=[f"pb{b}"], tag="xT")
            vop(lambda e, o=x_fm[:, 2048 * fc + 128 * tt:2048 * fc + 128 * tt + 128],
                i=pb(b, cn=128): e.tensor_copy(o, i),
                reads=[f"pb{b}"], writes=[f"x{fc}"], tag="xTe")

    LNTMP = [r2_f32[:, 2048 * i:2048 * i + 2048] for i in range(4)]
    XBF = lambda c: r2_bf[:, 2048 * c:2048 * c + 2048]
    R2_PREV = ["xt0", "xt1", "aout", "hact", "qprime"]
    R4_PREV = ["qga0", "qga1", "kga0", "kga1", "scr", "WvT", "wvStg"]

    def layernorm(ln_id):
        P.claim(["xbf", "lntmp0", "lntmp1", "lntmp2", "lntmp3"], R2_PREV)
        P.claim(["repA", "repB"], R4_PREV)
        for c in range(8):
            vop(lambda e, o=XBF(c), i=XC(c): e.tensor_copy(o, i),
                reads=[f"x{c}"], writes=["xbf"], tag=f"ln{ln_id}cast")
        for s in range(4):
            for c in range(8):
                mm(s, ps[0:1, 512 * s:512 * s + 512], ones_b[:, 0:1],
                   r2_bf[:, 2048 * c + 512 * s:2048 * c + 512 * s + 512],
                   first=(c == 0), last=(c == 7), reads=["ones_b", "xbf"],
                   tag=f"ln{ln_id}sx")
        for c in range(8):
            vop(lambda e, o=XNC(c), i=XBF(c): e.tensor_mul(o, i, i),
                reads=["xbf"], writes=["xnsq"], tag=f"ln{ln_id}sq")
        for s in range(4):
            for c in range(8):
                mm(4 + s, ps[0:1, 2048 + 512 * s:2048 + 512 * s + 512], ones_b[:, 0:1],
                   xn[:, 2048 * c + 512 * s:2048 * c + 512 * s + 512],
                   first=(c == 0), last=(c == 7), reads=["ones_b", "xnsq"],
                   tag=f"ln{ln_id}sx2")
        P.claim(["rw0", "rw1", "rw2"], ["xbf"])
        RW = [r2_f32[0:1, 0:2048], r2_f32[0:1, 2048:4096], r2_f32[0:1, 4096:6144]]
        vop(lambda e: e.tensor_scalar_mul(RW[0], ps[0:1, 0:2048], 1.0 / D),
            reads=["pb0", "pb1", "pb2", "pb3"], writes=["rw0"], tag=f"ln{ln_id}m")
        vop(lambda e: e.tensor_scalar_mul(RW[1], ps[0:1, 2048:4096], 1.0 / D),
            reads=["pb4", "pb5", "pb6", "pb7"], writes=["rw1"], tag=f"ln{ln_id}s2")
        vop(lambda e: e.tensor_mul(RW[2], RW[0], RW[0]),
            reads=["rw0"], writes=["rw2"], tag=f"ln{ln_id}m2")
        vop(lambda e: e.tensor_sub(RW[1], RW[1], RW[2]),
            reads=["rw1", "rw2"], writes=["rw1"], tag=f"ln{ln_id}var")
        sop(lambda e: e.activation(RW[2], RW[1], AF.Ln,
                                   bias=epsc[0:1, :], scale=1.0),
            reads=["rw1", "epsc"], writes=["rw2"], tag=f"ln{ln_id}ln")
        sop(lambda e: e.activation(RW[1], RW[2], AF.Exp,
                                   bias=zeroc[0:1, :], scale=-0.5),
            reads=["rw2", "zeroc"], writes=["rw1"], tag=f"ln{ln_id}exp")
        vop(lambda e: e.tensor_scalar_mul(RW[2], RW[0], -1.0),
            reads=["rw0"], writes=["rw2"], tag=f"ln{ln_id}negm")
        for nm, row, dst, dk in [("A", RW[1], repA, "rw1"),
                                 ("B", RW[2], repB, "rw2")]:
            for s in range(4):
                b = 4 + s
                mm(b, pb(b), ones_f[0:1, 0:128], row[:, 512 * s:512 * s + 512],
                   first=True, last=True, reads=["ones_f", dk],
                   tag=f"ln{ln_id}rep{nm}")
                vop(lambda e, o=dst[:, 512 * s:512 * s + 512], i=pb(b): e.tensor_copy(o, i),
                    reads=[f"pb{b}"], writes=[f"rep{nm}"], tag=f"ln{ln_id}repe{nm}")
        P.claim(["lntmp0", "lntmp1", "lntmp2", "lntmp3"], ["rw0", "rw1", "rw2"])
        for c in range(8):
            tmp = LNTMP[c % 4]
            vop(lambda e, o=tmp, a=XC(c): e.tensor_add(o, a, repB[:, :]),
                reads=[f"x{c}", "repB", f"lntmp{c % 4}"], writes=[f"lntmp{c % 4}"],
                tag=f"ln{ln_id}a")
            vop(lambda e, o=XNC(c), a=tmp: e.tensor_mul(o, a, repA[:, :]),
                reads=[f"lntmp{c % 4}", "repA", "xnsq"], writes=[f"xn{c}", "xnsq"],
                tag=f"ln{ln_id}b")

    def tap_sbuf(name, tensor, keys, cols):
        if tap is not None and tap[0] == name:
            for c0 in range(0, cols, 2048):
                dmaY(tap_t.ap()[:, c0:c0 + 2048], tensor[:, c0:c0 + 2048],
                     reads=list(keys), writes=["tapo"], tag="tap", force_inc=True)
            return True
        return False

    def tap_x(name):
        if tap is not None and tap[0] == name:
            for c in range(8):
                dmaY(tap_t.ap()[:, 2048 * c:2048 * c + 2048], XC(c),
                     reads=[f"x{c}"], writes=["tapo"], tag="tap", force_inc=True)
            return True
        return False

    def tap_xn(name):
        if tap is not None and tap[0] == name:
            for c in range(8):
                dmaY(tap_t.ap()[:, 2048 * c:2048 * c + 2048], XNC(c),
                     reads=[f"xn{c}"], writes=["tapo"], tag="tap", force_inc=True)
            return True
        return False

    def attn_a():
        P.claim(["qga0", "qga1", "kga0", "kga1", "scr", "qscr"], ["repA", "repB"])
        P.claim(["aout"], ["xbf", "lntmp0", "lntmp1", "lntmp2", "lntmp3", "xt0", "xt1"])
        for sl in range(2):
            P.op("G", lambda e, t=v_aug[sl]: e.memset(t[:, :], 1.0),
                 writes=[f"vga{sl}"], tag="vone")
        vop(lambda e: e.tensor_scalar_mul(bvr[0:1, :], bvr[0:1, :], 0.2),
            reads=["bvr"], writes=["bvr"], tag="bvr02", shortw=True)

        for lc in range(2):
            for ts in range(4):
                b = 2 + (ts % 2)
                for dm in range(8):
                    mm(b, pb(b), aWkd_s[:, 256 * dm + 128 * lc:256 * dm + 128 * lc + 128],
                       XNC(dm)[:, 512 * ts:512 * ts + 512],
                       first=(dm == 0), last=(dm == 7),
                       reads=["aWkd_s", f"xn{dm}"], tag="kvd")
                sop(lambda e, o=kvd_s[:, 2048 * lc + 512 * ts:2048 * lc + 512 * ts + 512],
                    i=pb(b), lc=lc: e.activation(o, i, AF.Identity, bias=bkdc[:, lc:lc + 1],
                                                 scale=1.0),
                    reads=[f"pb{b}", "bkdc"], writes=["kvd"], tag="kvde")

        def prep_gen(h):
            sl = h % 2
            qa, ka, va = q_aug[sl], k_aug[sl], v_aug[sl]
            qk, kk_, vk = f"qga{sl}", f"kga{sl}", f"vga{sl}"
            wqk = f"wqSl{sl}"
            wqb = 512 * sl
            dmaG(bass.AP(woSl[0], wqb, [[1024, 128], [64, 8], [1, 64]]),
                 bass.AP(W["aWq"], 64 * h, [[1024, 128], [131072, 8], [1, 64]]),
                 reads=[], writes=[wqk], tag=f"ldwq{h}")
            yield
            for half in range(2):
                hb = 1024 * half
                for sub in range(2):
                    b = 2 + sub
                    for lc in range(2):
                        mm(b, pb(b, p1=64),
                           aWku_s[:, 2048 * lc + 128 * h:2048 * lc + 128 * h + 64],
                           kvd_s[:, 2048 * lc + hb + 512 * sub:2048 * lc + hb + 512 * sub + 512],
                           first=(lc == 0), last=(lc == 1),
                           reads=["aWku_s", "kvd"], tag=f"kp{h}")
                    sop(lambda e, o=scr[0:64, hb + 512 * sub:hb + 512 * sub + 512],
                        i=pb(b, p1=64), h=h:
                        e.activation(o, i, AF.Identity, bias=bkuk[:, h:h + 1], scale=1.0),
                        reads=[f"pb{b}", "bkuk"], writes=["scr"], tag=f"kpe{h}")
                yield
                for sub in range(2):
                    b = 2 + sub
                    mm(b, pb(b, p1=64), prot[:, :],
                       scr[0:64, hb + 512 * sub:hb + 512 * sub + 512],
                       first=True, last=True, reads=["P_t", "scr"], tag=f"krot{h}")
                vop(lambda e, o=ka[0:64, hb:hb + 1024], i=scr[0:64, hb:hb + 1024],
                    c=C_s[0:64, hb:hb + 1024]: e.tensor_mul(o, i, c),
                    reads=["scr", "C_t"], writes=[kk_], tag=f"kr1{h}")
                vop(lambda e, o=scr[0:64, hb:hb + 1024],
                    i=ps[0:64, 1024:2048], s=S_s[0:64, hb:hb + 1024]:
                    e.tensor_mul(o, i, s),
                    reads=["pb2", "pb3", "S_t"], writes=["scr"], tag=f"kr2{h}")
                vop(lambda e, o=ka[0:64, hb:hb + 1024], i=scr[0:64, hb:hb + 1024]:
                    e.tensor_add(o, o, i),
                    reads=["scr", kk_], writes=[kk_], tag=f"kr3{h}")
                yield
            vop(lambda e, sl=sl: e.tensor_reduce(mkf[:, sl:sl + 1], ka[0:64, :],
                                                 AXX, OP.add),
                reads=[kk_], writes=[f"mkf{sl}"], tag=f"mk{h}", shortw=True)
            vop(lambda e, sl=sl: e.tensor_copy(mkcol[:, sl:sl + 1], mkf[:, sl:sl + 1]),
                reads=[f"mkf{sl}"], writes=[f"mk{sl}"], tag=f"mkc{h}", shortw=True)
            yield
            for half in range(2):
                hb = 1024 * half
                for sub in range(2):
                    b = 2 + sub
                    for dm in range(8):
                        mm(b, pb(b, p1=64),
                           woSl[0][:, wqb + 64 * dm:wqb + 64 * dm + 64],
                           XNC(dm)[:, hb + 512 * sub:hb + 512 * sub + 512],
                           first=(dm == 0), last=(dm == 7),
                           reads=[wqk, f"xn{dm}"], tag=f"qp{h}")
                    sop(lambda e, o=qscr[0:64, hb + 512 * sub:hb + 512 * sub + 512],
                        i=pb(b, p1=64), h=h:
                        e.activation(o, i, AF.Identity, bias=bqc64[:, h:h + 1], scale=0.125),
                        reads=[f"pb{b}", "bqc64"], writes=["qscr"], tag=f"qpe{h}")
                    yield
                for sub in range(2):
                    b = 2 + sub
                    mm(b, pb(b, p1=64), prot[:, :],
                       qscr[0:64, hb + 512 * sub:hb + 512 * sub + 512],
                       first=True, last=True, reads=["P_t", "qscr"], tag=f"qrot{h}")
                vop(lambda e, o=qa[0:64, hb:hb + 1024], i=qscr[0:64, hb:hb + 1024],
                    c=C_s[0:64, hb:hb + 1024]: e.tensor_mul(o, i, c),
                    reads=["qscr", "C_t"], writes=[qk], tag=f"qr1{h}")
                vop(lambda e, o=qscr[0:64, hb:hb + 1024],
                    i=ps[0:64, 1024:2048], s=S_s[0:64, hb:hb + 1024]:
                    e.tensor_mul(o, i, s),
                    reads=["pb2", "pb3", "S_t"], writes=["qscr"], tag=f"qr2{h}")
                vop(lambda e, o=qa[0:64, hb:hb + 1024], i=qscr[0:64, hb:hb + 1024]:
                    e.tensor_add(o, o, i),
                    reads=["qscr", qk], writes=[qk], tag=f"qr3{h}")
                yield
            for tt_ in range(16):
                b = 2 + tt_ // 8
                for lc in range(2):
                    mm(b, pb(b, c0=64 * (tt_ % 8), cn=64),
                       kvd_s[:, 2048 * lc + 128 * tt_:2048 * lc + 128 * tt_ + 128],
                       aWku_s[:, 2048 * lc + 128 * h + 64:2048 * lc + 128 * h + 128],
                       first=(lc == 0), last=(lc == 1),
                       reads=["kvd", "aWku_s"], tag=f"vp{h}")
            vop(lambda e, o=bass.AP(va, 0, [[1088, 128], [68, 16], [1, 64]]),
                i=ps[0:128, 1024:2048]: e.tensor_copy(o, i),
                reads=["pb2", "pb3"], writes=[vk], tag=f"vpe{h}")
            yield
            for tt_ in range(16):
                mm(2, ps[0:1, 1024:1024 + 65], ones_b[:, 0:1],
                   va[:, 68 * tt_:68 * tt_ + 65],
                   first=(tt_ == 0), last=(tt_ == 15), reads=["ones_b", vk],
                   tag=f"mv{h}")
            vop(lambda e, sl=sl: e.tensor_scalar_mul(mvrow[0:1, 68 * sl:68 * sl + 65],
                                              ps[0:1, 1024:1024 + 65], 0.2 / T),
                reads=["pb2"], writes=[f"mv{sl}"], tag=f"mve{h}", shortw=True)
            vop(lambda e, sl=sl, h=h: e.tensor_add(
                    mvrow[0:1, 68 * sl:68 * sl + 64],
                    mvrow[0:1, 68 * sl:68 * sl + 64], bvr[0:1, 64 * h:64 * h + 64]),
                reads=[f"mv{sl}", "bvr"], writes=[f"mv{sl}"], tag=f"mvb{h}", shortw=True)
            yield

        def attention(h, pgen):
            sl = h % 2
            qa, ka, va = q_aug[sl], k_aug[sl], v_aug[sl]
            qk, kk_, vk = f"qga{sl}", f"kga{sl}", f"vga{sl}"

            def issue_st(t):
                qp, kb, qs = t // 32, (t // 2) % 16, t % 2
                sb = t % 2
                slot = t % 3
                qc0 = 1024 * qp + 512 * qs
                mm(sb, pb(sb), ka[0:64, 128 * kb:128 * kb + 128],
                   qa[0:64, qc0:qc0 + 512], first=True, last=True,
                   reads=[kk_, qk], tag=f"st{h}")
                sop(lambda e, o=attE[slot][:, :], i=pb(sb):
                    e.activation(o, i, AF.Exp, bias=zeroc[:, :], scale=1.0),
                    reads=[f"pb{sb}", "zeroc"], writes=[f"attE{slot}"],
                    tag=f"ex{h}")
                d0 = (8 * qp + 4 * qs - kb) + 15
                if not _DEV_NOE:
                    vop(lambda e, o=attE[slot][:, :],
                        ee=E_s[:, 128 * d0:128 * d0 + 512]: e.tensor_mul(o, o, ee),
                        reads=[f"attE{slot}", "E_t"], writes=[f"attE{slot}"],
                        tag=f"em{h}")

            def issue_av(t):
                qp, kb, qs = t // 32, (t // 2) % 16, t % 2
                ob = 4 + 2 * qp + qs
                slot = t % 3
                mm(ob, pb(ob, p1=65), va[:, 68 * kb:68 * kb + 65],
                   attE[slot][:, :], first=(kb == 0), last=(kb == 15),
                   reads=[vk, f"attE{slot}"], tag=f"av{h}")

            def tail(qp):
                ob0 = 4 + 2 * qp
                qb0 = 1024 * qp
                for qs in range(2):
                    b = 2 + qs
                    mm(b, ps[0:1, 512 * b:512 * b + 512], mkcol[:, sl:sl + 1],
                       qa[0:64, qb0 + 512 * qs:qb0 + 512 * qs + 512],
                       first=True, last=True, reads=[f"mk{sl}", qk],
                       tag=f"ms{h}")
                sop(lambda e: e.activation(sglb[0:1, 0:1024], ps[0:1, 1024:2048],
                                           AF.Sigmoid, bias=zeroc[0:1, :],
                                           scale=8.0 / T),
                    reads=["pb2", "pb3", "zeroc", "sgl"], writes=["sgl"],
                    tag=f"sg{h}")
                vop(lambda e, i=ps[64:65, 512 * ob0:512 * ob0 + 1024]:
                    e.tensor_copy(sglb2[64:65, 0:1024], i),
                    reads=[f"pb{ob0}", f"pb{ob0 + 1}", "sgl2"], writes=["sgl2"],
                    tag=f"le{h}")
                for qs in range(2):
                    ob = ob0 + qs
                    mm(2, pb(2, p1=64), onesel[:, :],
                       sglb2[0:65, 512 * qs:512 * qs + 512],
                       first=True, last=True, reads=["onesel", "sgl2"],
                       tag=f"rp{h}")
                    sop(lambda e: e.activation(pb(3, p1=64), pb(2, p1=64),
                                               AF.Ln, bias=tinyc[0:64, :], scale=1.0),
                        reads=["pb2", "tinyc"], writes=["pb3"], tag=f"lr{h}")
                    sop(lambda e: e.activation(repsb[:, :], pb(3, p1=64),
                                               AF.Exp, bias=zeroc[0:64, :], scale=-1.0),
                        reads=["pb3", "zeroc", "repsb"], writes=["repsb"],
                        tag=f"rc{h}")
                    mm(3, pb(3, p1=64), mvrow[0:1, 68 * sl:68 * sl + 64],
                       sglb[0:1, 512 * qs:512 * qs + 512],
                       first=True, last=True, reads=[f"mv{sl}", "sgl"],
                       tag=f"ia{h}")
                    c0 = 2048 * (h // 2) + qb0 + 512 * qs
                    if h % 2 == 0:
                        vop(lambda e, o=r2_bf[0:64, c0:c0 + 512],
                            i=pb(ob, p1=64): e.tensor_mul(o, i, repsb[:, :]),
                            reads=[f"pb{ob}", "repsb"], writes=["aout"], tag=f"nz{h}")
                        vop(lambda e, o=r2_bf[0:64, c0:c0 + 512],
                            i2=pb(3, p1=64):
                            e.tensor_add(o, o, i2),
                            reads=["pb3", "aout"],
                            writes=["aout"], tag=f"nza{h}")
                    else:
                        vop(lambda e, i=pb(ob, p1=64):
                            e.tensor_mul(nztmp[:, :], i, repsb[:, :]),
                            reads=[f"pb{ob}", "repsb"], writes=["nzt"], tag=f"nz{h}")
                        vop(lambda e, i2=pb(3, p1=64):
                            e.tensor_add(nztmp[:, :], nztmp[:, :], i2),
                            reads=["pb3", "nzt"],
                            writes=["nzt"], tag=f"nza{h}")
                        dmaG(r2_bf[64:128, c0:c0 + 512], nztmp[:, :],
                             reads=["nzt"], writes=["aout"], tag=f"nzc{h}")

            LAG = 2
            for t in range(64):
                issue_st(t)
                if t % 4 == 2:
                    next(pgen, None)
                if t >= LAG:
                    issue_av(t - LAG)
                    if t - LAG == 31:
                        tail(0)
            for t in range(64 - LAG, 64):
                issue_av(t)
            for _ in pgen:
                pass
            tail(1)

        nh = _DEV_NHEADS or NH
        if tap is not None and tap[0] == "kvd":
            tap_sbuf("kvd", kvd_s, ["kvd"], 4096)
            return
        for _ in prep_gen(0):
            pass
        for h in range(nh):
            pgen = prep_gen(h + 1) if h + 1 < nh else iter(())
            attention(h, pgen)

        if tap is not None and tap[0] == "ps0":
            for b in range(8):
                vop(lambda e, b=b: e.tensor_copy(
                        r2_f32[:, 4096 + 512 * b:4096 + 512 * b + 512],
                        ps[:, 512 * b:512 * b + 512]),
                    reads=[f"pb{b}"], writes=["tapstg"], tag="tapc")
            for c0 in range(0, 4096, 2048):
                dmaY(tap_t.ap()[:, c0:c0 + 2048], r2_f32[:, 4096 + c0:6144 + c0],
                     reads=["tapstg"], writes=["tapo"], tag="tap",
                     force_inc=True)
            return
        if tap is not None and tap[0] == "tail0":
            stg = r2_f32[:, 4096:8192]
            def cp(dst0, src_ap, key):
                vop(lambda e,
                    o=stg[0:src_ap.shape[0], dst0:dst0 + src_ap.shape[-1]],
                    i=src_ap:
                    e.tensor_copy(o, i), reads=[key], writes=["tapstg"],
                    tag="tapc")
            cp(0, rowa[0:1, :], "rw64")
            cp(1024, sglb[0:1, :], "sgl")
            cp(2048, sglb2[64:65, :], "sgl2")
            cp(3072, mvrow[0:1, 0:64], "mv0")
            cp(3250, mkcol[0:64, 0:1], "mk0")
            cp(3252, mkf[0:64, 0:1], "mkf0")
            cp(3260, kmax2[0:1, 0:1], "km20")
            cp(3262, kmax2[0:1, 2:4], "km2s")
            cp(3270, onesel[0:65, :], "onesel")
            cp(3300, repsb[0:64, :], "repsb")
            for c0 in range(0, 4096, 2048):
                dmaY(tap_t.ap()[:, c0:c0 + 2048], stg[:, c0:c0 + 2048],
                     reads=["tapstg"], writes=["tapo"], tag="tap",
                     force_inc=True)
            return
        if tap is not None and tap[0] == "scr":
            dmaY(tap_t.ap()[0:64, :], scr[0:64, 0:2048],
                 reads=["scr"], writes=["tapo"], tag="tap", force_inc=True)
            return
        if tap is not None and tap[0] == "km":
            dmaY(tap_t.ap()[:, 0:4], kmax2[:, :],
                 reads=["km20", "km2s"], writes=["tapo"], tag="tap", force_inc=True, slow=True)
            return
        if tap is not None and tap[0] == "rwa":
            dmaY(tap_t.ap()[:, 0:1024], rowa[:, :],
                 reads=["rw64"], writes=["tapo"], tag="tap", force_inc=True)
            return
        if tap is not None and tap[0] == "wq0":
            dmaY(tap_t.ap()[:, 0:1024], woSl[0][:, 0:1024],
                 reads=["wqSl0", "wqSl1"], writes=["tapo"], tag="tap", force_inc=True)
            return
        if tap is not None and tap[0] in ("qa1", "ka1", "va1", "qa0", "ka0", "va0"):
            tsr = {"qa1": (q_aug[1], "qga1"), "ka1": (k_aug[1], "kga1"),
                   "va1": (v_aug[1], "vga1"), "qa0": (q_aug[0], "qga0"),
                   "ka0": (k_aug[0], "kga0"), "va0": (v_aug[0], "vga0")}[tap[0]]
            dmaY(tap_t.ap()[0:65, :], tsr[0][0:65, 0:1088] if tap[0].startswith("va")
                 else tsr[0][:, 0:2048],
                 reads=[tsr[1]], writes=["tapo"], tag="tap", force_inc=True)
            return
        if tap is not None and tap[0] == "aout":
            tap_sbuf("aout", r2_bf, ["aout"], 16384)
            return
        P.claim(["woSl0", "woSl1"], ["wqSl0", "wqSl1"])
        for oc in range(8):
            w = woSl[0]
            wk2 = "woSl0"
            dmaG(bass.AP(w, 0, [[1024, 128], [128, 8], [1, 128]]),
                 bass.AP(W["aWo"], 128 * oc, [[1024, 128], [131072, 8], [1, 128]]),
                 reads=[], writes=[wk2], tag=f"ldWo{oc}")
            for fc in range(8):
                mm(4, pb(4, cn=1), w[:, 128 * fc:128 * fc + 128], bvc[:, fc:fc + 1],
                   first=(fc == 0), last=(fc == 7), reads=[wk2, "bvc"], tag=f"bop{oc}")
            vop(lambda e, oc=oc: e.tensor_add(bopc[:, oc:oc + 1], pb(4, cn=1),
                                              boc[:, oc:oc + 1]),
                reads=["pb4", "boc"], writes=["bopc"], tag=f"bope{oc}", shortw=True)
            for ts in range(4):
                b = ts % 4
                for fc in range(8):
                    mm(b, pb(b), w[:, 128 * fc:128 * fc + 128],
                       r2_bf[:, 2048 * fc + 512 * ts:2048 * fc + 512 * ts + 512],
                       first=(fc == 0), last=(fc == 7), reads=[wk2, "aout"],
                       tag=f"wo{oc}")
                vop(lambda e, o=XC(oc)[:, 512 * ts:512 * ts + 512], i=pb(b), oc=oc:
                    e.scalar_tensor_tensor(o, i, bopc[:, oc:oc + 1], o, OP.add, OP.add),
                    reads=[f"pb{b}", "bopc", f"x{oc}"], writes=[f"x{oc}"], tag=f"woe{oc}")

    def ffn(fid):
        P.claim(["hact"], ["xbf", "lntmp0", "lntmp1", "lntmp2", "lntmp3"])
        P.claim(["fw10", "fw11", "fw20", "fw21"],
                ["wqSl0", "wqSl1", "woSl0", "woSl1", "aWkd_s", "aWku_s",
                 "bWk_s", "vtm"])
        HC = lambda icq, tt_: r2_bf[:, 4096 * tt_ + 512 * icq:4096 * tt_ + 512 * icq + 512]
        for q in range(4):
            for icq in range(8):
                ic = 8 * q + icq
                w = fW1b[ic % 2]
                wk2 = f"fw1{ic % 2}"
                dmaG(bass.AP(w, 0, [[1024, 128], [128, 8], [1, 128]]),
                     bass.AP(W["fW1"], 128 * ic, [[4096, 128], [524288, 8], [1, 128]]),
                     reads=[], writes=[wk2], tag=f"ldW1_{fid}")
                bs = 4 * (ic % 2)
                for hc in range(8):
                    for tt_ in range(4):
                        mm(bs + tt_, pb(bs + tt_), w[:, 128 * hc:128 * hc + 128],
                           XNC(hc)[:, 512 * tt_:512 * tt_ + 512],
                           first=(hc == 0), last=(hc == 7),
                           reads=[wk2, f"xn{hc}"], tag=f"f1_{fid}")
                for tt_ in range(4):
                    sop(lambda e, o=HC(icq, tt_), i=pb(bs + tt_), ic=ic:
                        e.activation(o, i, AF.Relu, bias=b1c[:, ic:ic + 1], scale=1.0),
                        reads=[f"pb{bs + tt_}", "b1c"], writes=["hact"],
                        tag=f"f1e_{fid}")
            for oc in range(8):
                w2 = fW2b[oc % 2]
                wk2 = f"fw2{oc % 2}"
                dmaG(bass.AP(w2, 0, [[4096, 128], [128, 8], [1, 128]]),
                     bass.AP(W["fW2"], 1048576 * q + 128 * oc,
                             [[1024, 128], [131072, 8], [1, 128]]),
                     reads=[], writes=[wk2], tag=f"ldW2_{fid}")
                bs = 4 * (oc % 2)
                for icq in range(8):
                    for tt_ in range(4):
                        mm(bs + tt_, pb(bs + tt_), w2[:, 128 * icq:128 * icq + 128],
                           HC(icq, tt_),
                           first=(icq == 0), last=(icq == 7), reads=[wk2, "hact"],
                           tag=f"f2_{fid}")
                for tt_ in range(4):
                    if q < 3:
                        vop(lambda e, o=XC(oc)[:, 512 * tt_:512 * tt_ + 512],
                            i=pb(bs + tt_):
                            e.tensor_add(o, i, o),
                            reads=[f"pb{bs + tt_}", f"x{oc}"], writes=[f"x{oc}"],
                            tag=f"f2e_{fid}")
                    else:
                        vop(lambda e, o=XC(oc)[:, 512 * tt_:512 * tt_ + 512],
                            i=pb(bs + tt_), oc=oc:
                            e.scalar_tensor_tensor(o, i, b2c[:, oc:oc + 1], o,
                                                   OP.add, OP.add),
                            reads=[f"pb{bs + tt_}", "b2c", f"x{oc}"],
                            writes=[f"x{oc}"], tag=f"f2e_{fid}")

    def attn_b():
        P.claim(["bWk_s"], ["fw10", "fw11", "fw20", "fw21"])
        P.claim(["A_t2"], ["C_t", "S_t", "E_t"])
        P.claim(["Gt"], ["kvd"])
        P.claim(["raug"], ["kvd"])
        P.claim(["qaug2"], ["vga0", "vga1", "attE0", "attE1", "attE2"])
        P.claim(["wvStg", "wvT2a", "wvT2b", "bqSl0", "bqSl1"],
                ["repA", "repB", "qga0", "qga1", "kga0", "kga1", "scr"])
        P.claim(["qprime"], ["xbf", "lntmp0", "lntmp1", "lntmp2", "lntmp3"])
        P.claim(["stb0", "stb1"], ["sgl", "nzt"])
        dmaG(Gt_s[:, :], CN["cG"].ap()[:, :], reads=[], writes=["Gt"], tag="ldG")
        dmaG(bass.AP(bWk_s, 0, [[8192, 128], [1024, 8], [1, 1024]]),
             bass.AP(W["bWk"], 0, [[1024, 128], [131072, 8], [1, 1024]]),
             reads=[], writes=["bWk_s"], tag="ldbWk")
        for h1c in range(8):
            w = bqSl[h1c % 2]
            wk2 = f"bqSl{h1c % 2}"
            dmaG(bass.AP(w, 0, [[1024, 128], [128, 8], [1, 128]]),
                 bass.AP(W["bWq"], 128 * h1c, [[1024, 128], [131072, 8], [1, 128]]),
                 reads=[], writes=[wk2], tag=f"ldbWq{h1c}")
            for half in range(2):
                b = (h1c % 2) * 2 + half
                for oc in range(8):
                    mm(b, pb(b), w[:, 128 * oc:128 * oc + 128],
                       bWk_s[:, 1024 * oc + 512 * half:1024 * oc + 512 * half + 512],
                       first=(oc == 0), last=(oc == 7), reads=[wk2, "bWk_s"],
                       tag=f"A{h1c}")
                vop(lambda e, o=A_s[:, 1024 * h1c + 512 * half:1024 * h1c + 512 * half + 512],
                    i=pb(b): e.tensor_copy(o, i),
                    reads=[f"pb{b}"], writes=["A_t2"], tag=f"Ae{h1c}")
            for oc in range(8):
                mm(4, pb(4, cn=1), w[:, 128 * oc:128 * oc + 128], bkbc[:, oc:oc + 1],
                   first=(oc == 0), last=(oc == 7), reads=[wk2, "bkbc"], tag=f"u{h1c}")
            vop(lambda e, h1c=h1c: e.tensor_copy(ucol[:, h1c:h1c + 1], pb(4, cn=1)),
                reads=["pb4"], writes=["ucol"], tag=f"ue{h1c}", shortw=True)
        for h2c in range(8):
            for oc in range(8):
                mm(5, pb(5, cn=1), bWk_s[:, 1024 * oc + 128 * h2c:1024 * oc + 128 * h2c + 128],
                   bqbc[:, oc:oc + 1], first=(oc == 0), last=(oc == 7),
                   reads=["bWk_s", "bqbc"], tag=f"w{h2c}")
            vop(lambda e, h2c=h2c: e.tensor_copy(wcol[:, h2c:h2c + 1], pb(5, cn=1)),
                reads=["pb5"], writes=["wcol"], tag=f"we{h2c}", shortw=True)
        for oc in range(8):
            mm(6, pb(6, p1=1, cn=1), bqbc[:, oc:oc + 1], bkbc[:, oc:oc + 1],
               first=(oc == 0), last=(oc == 7), reads=["bqbc", "bkbc"], tag="s0")
        vop(lambda e: e.tensor_copy(s0c[0:1, :], pb(6, p1=1, cn=1)),
            reads=["pb6"], writes=["s0c"], tag="s0e", shortw=True)

        for gc in range(8):
            for ts in range(4):
                b = ts % 4
                for hc in range(8):
                    mm(b, pb(b), A_s[:, 1024 * hc + 128 * gc:1024 * hc + 128 * gc + 128],
                       XNC(hc)[:, 512 * ts:512 * ts + 512],
                       first=(hc == 0), last=(hc == 7), reads=["A_t2", f"xn{hc}"],
                       tag=f"qp{gc}")
                sop(lambda e, o=r2_bf[:, 2048 * gc + 512 * ts:2048 * gc + 512 * ts + 512],
                    i=pb(b): e.activation(o, i, AF.Identity, bias=zeroc[:, :], scale=1.0),
                    reads=[f"pb{b}", "zeroc"], writes=["qprime"], tag=f"qpe{gc}")
        for nm, col, colk in [("rq", ucol, "ucol"), ("rk", wcol, "wcol")]:
            for s in range(4):
                b = 4 + s
                for hc in range(8):
                    mm(b, ps[0:1, 512 * b:512 * b + 512], col[:, hc:hc + 1],
                       XNC(hc)[:, 512 * s:512 * s + 512],
                       first=(hc == 0), last=(hc == 7), reads=[colk, f"xn{hc}"],
                       tag=nm)
            if nm == "rq":
                P.op("G", lambda e: e.memset(qaug2[0:2, :], 1.0), writes=["qaug2"],
                     tag="rqo")
                vop(lambda e: e.tensor_scalar(qaug2[0:1, :], ps[0:1, 2048:4096],
                                              s0c[0:1, :], None, OP.add),
                    reads=["pb4", "pb5", "pb6", "pb7", "s0c", "qaug2"],
                    writes=["qaug2"], tag="rqr")
            else:
                vop(lambda e: e.tensor_copy(raug[0:1, :], ps[0:1, 2048:4096]),
                    reads=["pb4", "pb5", "pb6", "pb7"], writes=["raug"], tag="rkc")
                dmaG(raug[1:2, :], raug[0:1, :], reads=["raug"], writes=["raug"],
                     tag="rkm")
                P.op("G", lambda e: e.memset(raug[0:1, :], 1.0), writes=["raug"],
                     tag="rko")
        P.claim(["vtm"], ["bWk_s", "A_t2", "fw10", "fw11", "fw20", "fw21", "onesel", "km20", "km21", "km2s", "mkf0", "mkf1", "mk0", "mk1"])
        for oc in range(8):
            dmaY(wvStg[:, :], W["bWv"].ap()[128 * oc:128 * oc + 128, :],
                 reads=["wvStg"], writes=["wvStg"], tag=f"ldWv{oc}")
            for hc in range(8):
                wsl = hc % 2
                P.op("T", lambda e, o=pb(3, cn=128), i=wvStg[:, 128 * hc:128 * hc + 128]:
                     e.transpose(o, i, ident[:, :]),
                     reads=["wvStg", "I_t"], writes=["pb3"], tag=f"wvT{oc}")
                vop(lambda e, o=wvT2[wsl][:, :], i=pb(3, cn=128): e.tensor_copy(o, i),
                    reads=["pb3"], writes=[f"wvT2{wsl}"], tag=f"wvTe{oc}")
                for tt_ in range(16):
                    b = 4 + tt_ // 4
                    mm(b, pb(b, c0=128 * (tt_ % 4), cn=128),
                       XNC(hc)[:, 128 * tt_:128 * tt_ + 128], wvT2[wsl][:, :],
                       first=(hc == 0 and tt_ % 4 == 0),
                       last=(hc == 7 and tt_ % 4 == 3),
                       reads=[f"xn{hc}", f"wvT2{wsl}"], tag=f"vt{oc}")
            for tt_ in range(16):
                b = 4 + tt_ // 4
                vop(lambda e, o=vtm[:, 1024 * tt_ + 128 * oc:1024 * tt_ + 128 * oc + 128],
                    i=pb(b, c0=128 * (tt_ % 4), cn=128), oc=oc: e.tensor_copy(o, i),
                    reads=[f"pb{b}"], writes=["vtm"], tag=f"vte{oc}")
        for s in range(8):
            q0 = 256 * s
            last_kb = 2 * s + 1
            for kb in range(last_kb + 1):
                sb = kb % 2
                slot = kb % 2
                for hc in range(8):
                    mm(sb, pb(sb, cn=256), XNC(hc)[:, 128 * kb:128 * kb + 128],
                       r2_bf[:, 2048 * hc + q0:2048 * hc + q0 + 256],
                       first=(hc == 0), last=False, reads=[f"xn{hc}", "qprime"],
                       tag=f"bs{s}")
                mm(sb, pb(sb, cn=256), raug[:, 128 * kb:128 * kb + 128],
                   qaug2[:, q0:q0 + 256], first=False, last=True,
                   reads=["raug", "qaug2"], tag=f"bs2{s}")
                d0 = 2 * s - kb + 1
                vop(lambda e, o=sglb[:, 256 * slot:256 * slot + 256], i=pb(sb, cn=256),
                    g=Gt_s[:, 128 * d0:128 * d0 + 256]: e.tensor_mul(o, i, g),
                    reads=[f"pb{sb}", "Gt"], writes=[f"stb{slot}"], tag=f"bm{s}")
                for hcb in range(8):
                    ob = 4 + hcb // 2
                    mm(ob, pb(ob, c0=256 * (hcb % 2), cn=256),
                       vtm[:, 1024 * kb + 128 * hcb:1024 * kb + 128 * hcb + 128],
                       sglb[:, 256 * slot:256 * slot + 256],
                       first=(kb == 0 and hcb % 2 == 0),
                       last=(kb == last_kb and hcb % 2 == 1),
                       reads=["vtm", f"stb{slot}"], tag=f"bv{s}")
            for hcb in range(8):
                ob = 4 + hcb // 2
                vop(lambda e, o=XC(hcb)[:, q0:q0 + 256],
                    i=pb(ob, c0=256 * (hcb % 2), cn=256), hcb=hcb:
                    e.tensor_add(o, i, o),
                    reads=[f"pb{ob}", f"x{hcb}"], writes=[f"x{hcb}"], tag=f"be{s}")

    def exit_phase():
        P.claim(["xt0", "xt1"], ["xbf", "lntmp0", "lntmp1", "lntmp2", "lntmp3",
                                 "hact", "qprime", "aout"])
        for tt_ in range(16):
            sl = tt_ % 2
            osb = xt_v[sl]
            for fc in range(8):
                b = fc % 4
                P.op("T", lambda e, o=pb(b, cn=128), i=XC(fc)[:, 128 * tt_:128 * tt_ + 128]:
                     e.transpose(o, i, ident[:, :]),
                     reads=[f"x{fc}", "I_t"], writes=[f"pb{b}"], tag="oT")
                vop(lambda e, o=osb[:, 128 * fc:128 * fc + 128], i=pb(b, cn=128):
                    e.tensor_copy(o, i),
                    reads=[f"pb{b}"], writes=[f"xt{sl}"], tag="oTe")
            dmaY(out_t.ap()[128 * tt_:128 * tt_ + 128, :], osb,
                 reads=[f"xt{sl}"], writes=[f"xt{sl}"], tag="out", force_inc=True)

    layernorm(1)
    done = tap_xn("xn1")
    if not done and n_phases >= 1:
        attn_a()
        done = tap_x("x1")
    if not done and n_phases >= 2:
        layernorm(2)
        ffn(1)
        done = tap_x("x2")
    if not done and n_phases >= 3:
        layernorm(3)
        attn_b()
        done = tap_x("x3")
    if not done and n_phases >= 4:
        layernorm(4)
        ffn(2)
        exit_phase()

    P.finalize()
    from contextlib import ExitStack
    ctx = ExitStack()
    keys = set(P.final_counts.keys()) | set(_ENGS)
    sems = {k: ctx.enter_context(nc.semaphore("sm_" + k.replace(":", "_")))
            for k in sorted(keys)}
    block = ctx.enter_context(nc.Block())
    P.emit(nc, block, sems)
    ctx.close()
    return nc

def kernel(**inputs):
    cons = _consts()
    nc = _build()
    x = np.ascontiguousarray(inputs["L0_attn_a_x"], dtype=np.float32)
    base = {
        "aWq": inputs["L0_attn_a_Wq"], "abq": inputs["L0_attn_a_bq"],
        "aWkd": inputs["L0_attn_a_Wkv_down"], "abkd": inputs["L0_attn_a_bkv_down"],
        "aWku": inputs["L0_attn_a_Wkv_up"], "abku": inputs["L0_attn_a_bkv_up"],
        "aWo": inputs["L0_attn_a_Wo"], "abo": inputs["L0_attn_a_bo"],
        "bWq": inputs["L0_attn_b_Wq"], "bbq": inputs["L0_attn_b_bq"],
        "bWk": inputs["L0_attn_b_Wk"], "bbk": inputs["L0_attn_b_bk"],
        "bWv": inputs["L0_attn_b_Wv"],
        "fW1": inputs["L0_ffn_W1"], "fb1": inputs["L0_ffn_b1"],
        "fW2": inputs["L0_ffn_W2"], "fb2": inputs["L0_ffn_b2"],
    }
    base = {k: np.ascontiguousarray(v, dtype=np.float32) for k, v in base.items()}
    base.update(cons)
    in_maps = []
    for c in range(NC_):
        m = dict(base)
        m["x"] = np.ascontiguousarray(x[c])
        in_maps.append(m)
    res = run_bass_kernel_spmd(nc, in_maps, core_ids=list(range(NC_)))
    return np.stack([r["out"] for r in res.results], axis=0)
